# version 11
# baseline (speedup 1.0000x reference)
import contextlib

import numpy as np

import concourse.bass as bass
import concourse.mybir as mybir
from concourse.bass_utils import run_bass_kernel_spmd

F32 = mybir.dt.float32
BF16 = mybir.dt.bfloat16
AF = mybir.ActivationFunctionType
ALU = mybir.AluOpType
AX = mybir.AxisListType

ENGINES = ("tensor", "vector", "scalar", "sync", "gpsimd")
D = 1024
NH = 16
HD = 64
DFF0 = 4096
P_TREE = 4
G_TREE = 1365
DEC_G = 341
DEC_W = DEC_G * 4
HID1 = P_TREE * G_TREE * 4
PBLK = G_TREE * 4

C32 = 12582912.0
C64 = 6755399441055744.0
NEGBIG = -1.0e30
TPH = 6.2831854820251465
TPL = -1.7484556025237907e-07
PHH = 1.5707963705062866
PHL = -4.371139000186241e-08
PICLIP = 3.1415925


class _Op:
    __slots__ = ("eng", "fn", "reads", "writes", "dma", "deps", "inc")

    def __init__(self, eng, fn, reads, writes, dma):
        self.eng, self.fn = eng, fn
        self.reads, self.writes = tuple(reads), tuple(writes)
        self.dma = dma
        self.deps = set()
        self.inc = None


class KB:

    def __init__(self, nc):
        self.nc = nc
        self.ops = []
        self.buf = {}

    def _bs(self, key):
        st = self.buf.get(key)
        if st is None:
            st = {"w": None, "r": {}, "p": set()}
            self.buf[key] = st
        return st

    def fence(self, src_keys, dst_keys):
        deps = set()
        for sk in src_keys:
            st = self._bs(sk)
            if st["w"] is not None:
                deps.add(st["w"])
            deps.update(st["r"].values())
        for dk in dst_keys:
            self._bs(dk)["p"].update(deps)

    def op(self, eng, fn, reads=(), writes=(), dma=False):
        idx = len(self.ops)
        if dma:
            dma = "ch_" + (writes[0] if writes else reads[0])
        o = _Op(eng, fn, reads, writes, dma)
        for b in o.reads:
            st = self._bs(b)
            if st["p"]:
                o.deps.update(st["p"])
                st["p"] = set()
            if st["w"] is not None:
                o.deps.add(st["w"])
        for b in o.writes:
            st = self._bs(b)
            if st["p"]:
                o.deps.update(st["p"])
                st["p"] = set()
            if st["w"] is not None:
                o.deps.add(st["w"])
            o.deps.update(st["r"].values())
        o.deps.discard(idx)
        self.ops.append(o)
        rk = eng + ("D" if dma else "")
        for b in o.reads:
            self._bs(b)["r"][rk] = idx
        for b in o.writes:
            st = self._bs(b)
            st["w"] = idx
            st["r"] = {}
        return idx

    def dma(self, eng, out, in_, reads=(), writes=()):
        return self.op(eng, lambda e, out=out, in_=in_: e.dma_start(out=out, in_=in_),
                       reads=reads, writes=writes, dma=True)

    def build(self):
        needed = set()
        for idx, o in enumerate(self.ops):
            needed.update(o.deps)
            if o.dma:
                needed.add(idx)
        counters = {}
        for idx in sorted(needed):
            o = self.ops[idx]
            s = o.dma if o.dma else ("P_" + o.eng)
            step = 16 if o.dma else 1
            counters[s] = counters.get(s, 0) + step
            o.inc = (s, counters[s])
        stack = contextlib.ExitStack()
        sems = {s: stack.enter_context(self.nc.semaphore(s)) for s in counters}
        hw = {}
        streams = {e: [] for e in ENGINES}
        for o in self.ops:
            waits = {}
            for d in o.deps:
                s, v = self.ops[d].inc
                if hw.get((o.eng, s), 0) >= v:
                    continue
                if waits.get(s, 0) < v:
                    waits[s] = v
            for s, v in sorted(waits.items()):
                streams[o.eng].append(lambda e, h=sems[s], v=v: e.wait_ge(h, v))
                hw[(o.eng, s)] = max(hw.get((o.eng, s), 0), v)
            if o.inc is not None:
                s, _ = o.inc
                step = 16 if o.dma else 1
                streams[o.eng].append(
                    lambda e, fn=o.fn, h=sems[s], n=step: fn(e).then_inc(h, n))
            else:
                streams[o.eng].append(o.fn)
        last_dma = None
        for o in reversed(self.ops):
            if o.dma and o.inc is not None:
                last_dma = o.inc
                break
        with self.nc.Block() as block:
            for eng_name in ENGINES:
                st = streams[eng_name]
                if not st and eng_name != "sync":
                    continue

                def make(st, eng_name=eng_name):
                    def _(eng):
                        for fn in st:
                            fn(eng)
                        if eng_name == "sync" and last_dma is not None:
                            eng.wait_ge(sems[last_dma[0]], last_dma[1])
                    return _
                getattr(block, eng_name)(make(st))
        return stack


def ts(i, n):
    return slice(i * n, (i + 1) * n)


def l0_chunk_off(h):
    if h < 4:
        return h, 0
    if h < 8:
        return h - 4, 64
    if h < 12:
        return h - 4, 0
    return h - 8, 64


def l0_wqb_perm():
    perm = np.zeros(D, dtype=np.int64)
    for h in range(NH):
        c, o = l0_chunk_off(h)
        perm[c * 128 + o: c * 128 + o + 64] = np.arange(h * 64, h * 64 + 64)
    return perm


def build(T=2048, n_cores=8):
    nc = bass.Bass("TRN2", target_bir_lowering=False)
    k = KB(nc)
    NT = T // 128
    NTG = T // 512
    assert T % 512 == 0

    def dram(name, shape, kind=None):
        if kind:
            return nc.dram_tensor(name, shape, F32, kind=kind).ap()
        return nc.dram_tensor(name, shape, F32).ap()

    x_d = dram("x", [T, D], "ExternalInput")
    wqa_d = dram("wqa", [D, 64], "ExternalInput")
    wqb_d = dram("wqb", [64, D], "ExternalInput")
    wkva_d = dram("wkva", [D, 32], "ExternalInput")
    wkvb_d = dram("wkvb", [32, 512], "ExternalInput")
    wout0_d = dram("wout0", [D, D], "ExternalInput")
    bout0_d = dram("bout0", [128, 8], "ExternalInput")
    winT0_d = dram("winT0", [DFF0, D], "ExternalInput")
    bin0_d = dram("bin0", [128, 32], "ExternalInput")
    wo0_d = dram("wo0", [DFF0, D], "ExternalInput")
    bo0bc_d = dram("bo0bc", [128, D], "ExternalInput")
    wqT_d = dram("wqT", [D, D], "ExternalInput")
    wkT_d = dram("wkT", [D, D], "ExternalInput")
    wvT_d = dram("wvT", [D, D], "ExternalInput")
    woT_d = dram("woT", [D, D], "ExternalInput")
    w1T_d = dram("w1T", [D, HID1], "ExternalInput")
    b1bc_d = dram("b1bc", [128, HID1], "ExternalInput")
    w2T_d = dram("w2T", [HID1, D], "ExternalInput")
    ident_d = dram("ident", [128, 128], "ExternalInput")
    out_d = dram("out", [T, D], "ExternalOutput")

    x1sp = dram("x1sp", [T, D])
    xn1sp = dram("xn1sp", [T, D])
    ssp = dram("ssp", [T, D])
    hidsp = dram("hidsp", [DFF0, T])
    wintsp = dram("wintsp", [D, DFF0])
    wpsp = dram("wpsp", [DFF0, D])
    qlatsp = dram("qlatsp", [96, T])
    cc_in_a = dram("cc_in_a", [1, 1])
    cc_out_a = dram("cc_out_a", [1, 1])
    cc_in_b = dram("cc_in_b", [1, 1])
    cc_out_b = dram("cc_out_b", [1, 1])

    def sb(name, shape, dt=F32):
        return nc.alloc_sbuf_tensor(name, shape, dt).ap()

    AW = max(8 * T, 16384)
    A1 = sb("A1", [128, AW])
    A2 = sb("A2", [128, AW])
    ident = sb("ident_s", [128, 128])
    maskd = sb("maskd", [128, 128])
    tiles = sb("tiles", [128, 4096])
    tile_a, tile_b = tiles[:, 0:1024], tiles[:, 1024:2048]
    tile_c, tile_d = tiles[:, 2048:3072], tiles[:, 3072:4096]
    TILEK = ["tile_a", "tile_b", "tile_c", "tile_d"]
    TILEV = [tile_a, tile_b, tile_c, tile_d]
    stg = [sb(f"stg{i}", [128, 1024]) for i in range(2)]
    wsm = [sb(f"wsm{i}", [128, 1024]) for i in range(2)]
    pstrip = sb("pstrip", [128, max(T, 2048)])
    ptstrip = sb("ptstrip", [128, max(T, 2048)])
    qTt = sb("qTt", [128, 1024])
    wqa_s = sb("wqa_s", [128, 8 * 64])
    wkva_s = sb("wkva_s", [128, 8 * 32])
    bout0_s = sb("bout0_s", [128, 8])
    bin0_s = sb("bin0_s", [128, 32])
    scal = sb("scal", [128, 16])
    stats = sb("stats", [128, 16])
    lnm = sb("lnm", [128, 2 * NT])
    srow = sb("srow", [1, 64])
    srow2 = sb("srow2", [1, 128])
    ones1 = sb("ones1", [1, 128])
    invf = sb("invf", [128, 32])
    tcol = sb("tcol", [128, 1])
    scmul = sb("scmul", [128, 32])
    wstat = sb("wstat", [128, 8])
    rsc = sb("rsc", [128, 64])
    recips = sb("recips", [128, 16])
    mask1h = nc.alloc_sbuf_tensor("mask1h", [128, P_TREE * G_TREE], BF16).ap()
    treb = nc.alloc_sbuf_tensor("treb", [128, 4 * P_TREE * DEC_G], BF16).ap()
    tre = [treb[:, ts(i, P_TREE * DEC_G)] for i in range(4)]

    ktl0 = A2[:, 0:2 * T]
    vl0 = A2[:, 2 * T:2 * T + NT * 256]
    cosq = A2[:, 6 * T:6 * T + NT * 32]
    sinq = A2[:, 6 * T + NT * 32:6 * T + 2 * NT * 32]
    bo0bc_s = A2[:, 0:D]
    qlat2 = A2[0:64, 8192:8192 + T]
    kvlat2 = A2[64:96, 8192:8192 + T]
    wqb_s = A2[0:64, 8192 + T:8192 + T + D]
    wkvb_s = A2[64:96, 8192 + T + D:8192 + T + D + 512]
    actdec = A1[:, 0:P_TREE * DEC_W]
    trm = A1[:, 5456:5456 + P_TREE * DEC_G]
    trc = A1[:, 6820:6820 + P_TREE * DEC_G]
    trt = A1[:, 8184:8184 + P_TREE * DEC_G]
    bslc = [A1[:, 9600 + 512 * i: 9600 + 512 * (i + 1)] for i in range(2)]
    aslc = [A1[:, 10624 + 512 * i: 10624 + 512 * (i + 1)] for i in range(2)]
    mstg = [A1[:, 11648 + 512 * i: 11648 + 512 * (i + 1)] for i in range(2)]
    xn1t = A1[:, 12672:12672 + 1024]
    wbufA = [A2[:, ts(i, 4096)] for i in range(2)]
    wbufB = [A2[:, 8192 + 4096 * i: 8192 + 4096 * (i + 1)] for i in range(2)]
    P9C_A1 = ["actdec", "trm", "trc", "trt", "bslc0", "bslc1", "aslc0", "aslc1",
              "mstg0", "mstg1", "xn1t"]
    P9C_A2 = ["wbufA0", "wbufA1", "wbufB0", "wbufB1"]

    psA = nc.alloc_psum_tensor("psA", [128, 2048], F32).ap()
    psB = nc.alloc_psum_tensor("psB", [128, 1024], F32).ap()
    psC = nc.alloc_psum_tensor("psC", [128, 512], F32).ap()
    psD = nc.alloc_psum_tensor("psD", [128, 512], F32).ap()
    PSA = ["psA0", "psA1", "psA2", "psA3"]
    PSB = ["psB0", "psB1"]

    V, S, TE, SY, GP = "vector", "scalar", "tensor", "sync", "gpsimd"

    def t_copy(eng, out, in_, reads, writes):
        k.op(eng, lambda e: e.tensor_copy(out, in_), reads=reads, writes=writes)

    def emit_round(buf, scr, key, skey):
        k.op(V, lambda e: e.tensor_scalar(scr, buf, C32, C32, ALU.add, ALU.subtract),
             reads=[key], writes=[skey])
        k.op(V, lambda e: e.tensor_scalar(buf, buf, C64, C64, ALU.add, ALU.subtract),
             reads=[key], writes=[key])
        k.op(V, lambda e: e.tensor_scalar(buf, buf, scal[:, 1:2], None, ALU.mult),
             reads=[key, "w64"], writes=[key])
        k.op(V, lambda e: e.scalar_tensor_tensor(buf, scr, scal[:, 0:1], buf,
                                                 ALU.mult, ALU.add),
             reads=[skey, key, "w32"], writes=[key])

    def bc_scalar(src_ap, src_key, dst_col):
        k.op(TE, lambda e: e.matmul(psD[:, 0:1], ones1, src_ap, start=True, stop=True),
             reads=["ones1", src_key], writes=["psD"])
        t_copy(V, scal[:, dst_col:dst_col + 1], psD[:, 0:1], ["psD"],
               [f"scal{dst_col}"])

    def part_max(col_ap, col_key, out_ap, out_key):
        k.op(TE, lambda e: e.transpose(psD[0:1, 0:128], col_ap, ident),
             reads=[col_key, "ident"], writes=["psD"])
        t_copy(V, srow2, psD[0:1, 0:128], ["psD"], ["srow2"])
        k.op(V, lambda e: e.reduce_max(out_ap, srow2, axis=AX.X),
             reads=["srow2"], writes=[out_key])

    def xnT_ap(buf, c, j0, n):
        assert j0 % 128 == 0 and n % 128 == 0
        return bass.AP(buf.tensor, buf.offset + (j0 // 128) * 1024 + 128 * c,
                       [[buf.ap[0][0], 128], [1024, n // 128], [1, 128]])

    def mm_group(out_ap, pairs, psum_key, read_keys):
        def fn(e, pairs=pairs, out_ap=out_ap):
            ins = None
            n = len(pairs)
            for i, (lh, rh) in enumerate(pairs):
                ins = e.matmul(out_ap, lh, rh, start=(i == 0), stop=(i == n - 1))
            return ins
        k.op(TE, fn, reads=list(read_keys), writes=[psum_key])

    def emit_transpose_tile(src, src_key, dst, dst_key):
        for half in range(2):
            for c4 in range(4):
                c = 4 * half + c4
                k.op(TE, lambda e, c=c, c4=c4, src=src: e.transpose(
                    psC[:, ts(c4, 128)], src[:, ts(c, 128)], ident),
                     reads=[src_key, "ident"], writes=["psC"])
            k.op(V, lambda e, half=half, dst=dst: e.tensor_copy(dst[:, ts(half, 512)],
                                                                psC),
                 reads=["psC"], writes=[dst_key])

    k.dma(SY, ident, ident_d, writes=["ident"])
    k.dma(SY, bout0_s, bout0_d, writes=["bout0"])
    k.dma(SY, bin0_s, bin0_d, writes=["bin0"])
    k.dma(SY, wqa_s, bass.AP(wqa_d.tensor, 0, [[64, 128], [64 * 128, 8], [1, 64]]),
          writes=["wqa"])
    k.dma(SY, wkva_s, bass.AP(wkva_d.tensor, 0, [[32, 128], [32 * 128, 8], [1, 32]]),
          writes=["wkva"])

    k.op(GP, lambda e: e.memset(ones1, 1.0), writes=["ones1"])
    k.op(GP, lambda e: e.memset(scal[0:1, 6:7], 1.0), writes=["sc6"])
    k.op(V, lambda e: e.tensor_scalar(scal[0:1, 7:8], scal[0:1, 6:7],
                                      9.313225746154785e-10, 1.0, ALU.add,
                                      ALU.subtract),
         reads=["sc6"], writes=["sc7"])
    k.op(V, lambda e: e.tensor_scalar(scal[0:1, 7:8], scal[0:1, 7:8],
                                      1073741824.0, None, ALU.mult),
         reads=["sc7"], writes=["sc7"])
    k.op(V, lambda e: e.tensor_scalar(scal[0:1, 8:9], scal[0:1, 7:8],
                                      -1.0, 1.0, ALU.mult, ALU.add),
         reads=["sc7"], writes=["sc8"])
    k.op(TE, lambda e: e.matmul(psD[:, 0:2], ones1, scal[0:1, 7:9], start=True,
                                stop=True),
         reads=["ones1", "sc7", "sc8"], writes=["psD"])
    t_copy(V, scal[:, 1:2], psD[:, 0:1], ["psD"], ["w64"])
    t_copy(V, scal[:, 0:1], psD[:, 1:2], ["psD"], ["w32"])

    k.op(GP, lambda e: e.iota(maskd, [[1, 128]], channel_multiplier=-1,
                              allow_small_or_imprecise_dtypes=True),
         writes=["maskd"])
    k.op(V, lambda e: e.tensor_scalar(maskd, maskd, 0.5, NEGBIG, ALU.is_gt, ALU.mult),
         reads=["maskd"], writes=["maskd"])

    for t in range(NT):
        k.dma(SY, tile_a, x_d[ts(t, 128), :], writes=["tile_a"])
        k.op(V, lambda e: e.reduce_sum(stats[:, 0:1], tile_a, axis=AX.X),
             reads=["tile_a"], writes=["st0"])
        k.op(S, lambda e: e.activation(tile_c, tile_a, AF.Square,
                                       accum_out=stats[:, 1:2]),
             reads=["tile_a"], writes=["tile_c", "st1"])
        k.op(V, lambda e: e.tensor_scalar(stats[:, 2:3], stats[:, 0:1], 1.0 / D, None,
                                          ALU.mult),
             reads=["st0"], writes=["st2"])
        k.op(V, lambda e: e.tensor_scalar(stats[:, 4:5], stats[:, 2:3], stats[:, 2:3],
                                          None, ALU.mult),
             reads=["st2"], writes=["st4"])
        k.op(V, lambda e: e.tensor_scalar(stats[:, 5:6], stats[:, 1:2], 1.0 / D, 1e-5,
                                          ALU.mult, ALU.add),
             reads=["st1"], writes=["st5"])
        k.op(V, lambda e: e.tensor_sub(stats[:, 5:6], stats[:, 5:6], stats[:, 4:5]),
             reads=["st5", "st4"], writes=["st5"])
        k.op(V, lambda e: e.reciprocal(stats[:, 6:7], stats[:, 5:6]),
             reads=["st5"], writes=["st6"])
        k.op(S, lambda e: e.activation(stats[:, 7:8], stats[:, 6:7], AF.Sqrt),
             reads=["st6"], writes=["st7"])
        k.op(V, lambda e: e.tensor_scalar(tile_b, tile_a, stats[:, 2:3], stats[:, 7:8],
                                          ALU.subtract, ALU.mult),
             reads=["tile_a", "st2", "st7"], writes=["tile_b"])
        k.op(V, lambda e: e.reduce_max(stats[:, 8:9], tile_b, axis=AX.X,
                                       apply_absolute_value=True),
             reads=["tile_b"], writes=["st8"])
        if t == 0:
            t_copy(V, stats[:, 9:10], stats[:, 8:9], ["st8"], ["st9"])
        else:
            k.op(V, lambda e: e.tensor_max(stats[:, 9:10], stats[:, 9:10],
                                           stats[:, 8:9]),
                 reads=["st9", "st8"], writes=["st9"])
        for half in range(2):
            for c4 in range(4):
                c = 4 * half + c4
                k.op(TE, lambda e, c=c, c4=c4: e.transpose(
                    psC[:, ts(c4, 128)], tile_b[:, ts(c, 128)], ident),
                     reads=["tile_b", "ident"], writes=["psC"])
            k.op(V, lambda e, t=t, half=half: e.tensor_copy(
                A1[:, 1024 * t + 512 * half: 1024 * t + 512 * half + 512], psC),
                 reads=["psC"], writes=["A1"])

    part_max(stats[:, 9:10], "st9", stats[0:1, 10:11], "st10")
    k.op(V, lambda e: e.tensor_scalar(stats[0:1, 10:11], stats[0:1, 10:11], 1e-8,
                                      None, ALU.max),
         reads=["st10"], writes=["st10"])
    k.dma(GP, cc_in_a, stats[0:1, 10:11], reads=["st10"], writes=["cca"])
    k.op(GP, lambda e: e.collective_compute(
        "AllReduce", ALU.max, replica_groups=[list(range(n_cores))],
        ins=[cc_in_a.opt()], outs=[cc_out_a.opt()]),
         reads=["cca"], writes=["ccao"])
    k.dma(GP, stats[0:1, 11:12], cc_out_a, reads=["ccao"], writes=["st11"])
    k.op(V, lambda e: e.reciprocal(stats[0:1, 12:13], stats[0:1, 11:12]),
         reads=["st11"], writes=["st12"])
    k.op(V, lambda e: e.tensor_scalar(stats[0:1, 12:13], stats[0:1, 12:13], 32767.0,
                                      None, ALU.mult),
         reads=["st12"], writes=["st12"])
    k.op(V, lambda e: e.reciprocal(stats[0:1, 13:14], stats[0:1, 12:13]),
         reads=["st12"], writes=["st13"])
    bc_scalar(stats[0:1, 12:13], "st12", 2)
    bc_scalar(stats[0:1, 13:14], "st13", 3)

    for sl in range(NTG):
        pairs = [(wqa_s[:, ts(c, 64)], xnT_ap(A1, c, sl * 512, 512)) for c in range(8)]
        mm_group(psD[0:64, :], pairs, "psD", ["wqa", "A1"])
        t_copy(V, stg[0][0:64, 0:512], psD[0:64, :], ["psD"], ["stg0"])
        k.dma(SY, qlatsp[0:64, ts(sl, 512)], stg[0][0:64, 0:512],
              reads=["stg0"], writes=["qlatsp"])
        pairs = [(wkva_s[:, ts(c, 32)], xnT_ap(A1, c, sl * 512, 512)) for c in range(8)]
        mm_group(psC[0:32, :], pairs, "psC", ["wkva", "A1"])
        t_copy(V, stg[1][0:32, 0:512], psC[0:32, :], ["psC"], ["stg1"])
        k.dma(SY, qlatsp[64:96, ts(sl, 512)], stg[1][0:32, 0:512],
              reads=["stg1"], writes=["qlatsp"])

    for j in range(8 * T // 1024):
        sl = A2[:, ts(j, 1024)]
        src = A1[:, ts(j, 1024)]
        k.op(V, lambda e, sl=sl, src=src: e.tensor_scalar(sl, src, scal[:, 2:3], None,
                                                          ALU.mult),
             reads=["A1", "scal2"], writes=["A2"])
        emit_round(sl, tile_c, "A2", "tile_c")
        k.op(V, lambda e, sl=sl: e.tensor_scalar(sl, sl, 32767.0, -32768.0,
                                                 ALU.min, ALU.max),
             reads=["A2"], writes=["A2"])

    for c in range(32):
        w = wsm[c % 2]
        wk = f"wsm{c % 2}"
        k.dma(SY, w, winT0_d[ts(c, 128), :], writes=[wk])
        k.op(V, lambda e, w=w: e.reduce_max(wstat[:, 0:1], w, axis=AX.X,
                                            apply_absolute_value=True),
             reads=[wk], writes=["ws0"])
        t_copy(V, scmul[:, c:c + 1], wstat[:, 0:1], ["ws0"], ["scmul"])
        k.op(V, lambda e: e.reciprocal(wstat[:, 1:2], wstat[:, 0:1]),
             reads=["ws0"], writes=["ws1"])
        k.op(V, lambda e: e.tensor_scalar(wstat[:, 1:2], wstat[:, 1:2], 127.0, None,
                                          ALU.mult),
             reads=["ws1"], writes=["ws1"])
        k.op(V, lambda e, w=w: e.tensor_scalar(w, w, wstat[:, 1:2], None, ALU.mult),
             reads=[wk, "ws1"], writes=[wk])
        emit_round(w, stg[0], wk, "stg0")
        k.op(V, lambda e, w=w: e.tensor_scalar(w, w, 127.0, -128.0, ALU.min, ALU.max),
             reads=[wk], writes=[wk])
        for half in range(2):
            for c4 in range(4):
                cc = 4 * half + c4
                k.op(TE, lambda e, w=w, cc=cc, c4=c4: e.transpose(
                    psC[:, ts(c4, 128)], w[:, ts(cc, 128)], ident),
                     reads=[wk, "ident"], writes=["psC"])
            k.op(V, lambda e, half=half: e.tensor_copy(stg[1][:, ts(half, 512)], psC),
                 reads=["psC"], writes=["stg1"])
        k.dma(SY, bass.AP(wintsp.tensor, 128 * c,
                          [[DFF0, 128], [128 * DFF0, 8], [1, 128]]),
              stg[1].rearrange("p (a b) -> p a b", b=128),
              reads=["stg1"], writes=["wintsp"])
    k.op(V, lambda e: e.tensor_scalar(scmul, scmul, scal[:, 3:4], 1.0 / 127.0,
                                      ALU.mult, ALU.mult),
         reads=["scmul", "scal3"], writes=["scmul"])

    for grp in range(NTG):
        for m in range(32):
            w = wsm[m % 2]
            wk = f"wsm{m % 2}"
            k.dma(SY, w, bass.AP(wintsp.tensor, 128 * m,
                                 [[DFF0, 128], [128 * DFF0, 8], [1, 128]]),
                  reads=["wintsp"], writes=[wk])
            ps, pk = (psC, "psC") if m % 2 == 0 else (psD, "psD")
            pairs = [(w[:, ts(c, 128)], xnT_ap(A2, c, grp * 512, 512))
                     for c in range(8)]
            mm_group(ps, pairs, pk, [wk, "A2"])
            st_ = stg[m % 2]
            sk = f"stg{m % 2}"
            k.op(S, lambda e, ps=ps, st_=st_, m=m: e.activation(
                st_[:, 0:512], ps, AF.Gelu, bias=bin0_s[:, m:m + 1],
                scale=scmul[:, m:m + 1]),
                 reads=[pk, "bin0", "scmul"], writes=[sk])
            k.op(V, lambda e, st_=st_: e.reduce_max(wstat[:, 2:3], st_[:, 0:512],
                                                    axis=AX.X,
                                                    apply_absolute_value=True),
                 reads=[sk], writes=["ws2"])
            if grp == 0 and m == 0:
                t_copy(V, wstat[:, 3:4], wstat[:, 2:3], ["ws2"], ["ws3"])
            else:
                k.op(V, lambda e: e.tensor_max(wstat[:, 3:4], wstat[:, 3:4],
                                               wstat[:, 2:3]),
                     reads=["ws3", "ws2"], writes=["ws3"])
            k.dma(SY, bass.AP(hidsp.tensor, 128 * m * T + 512 * grp,
                              [[T, 128], [1, 512]]),
                  st_[:, 0:512], reads=[sk], writes=["hidsp"])

    part_max(wstat[:, 3:4], "ws3", wstat[0:1, 4:5], "ws4")
    k.op(V, lambda e: e.tensor_scalar(wstat[0:1, 4:5], wstat[0:1, 4:5], 1e-8, None,
                                      ALU.max),
         reads=["ws4"], writes=["ws4"])
    k.dma(GP, cc_in_b, wstat[0:1, 4:5], reads=["ws4"], writes=["ccb"])
    k.op(GP, lambda e: e.collective_compute(
        "AllReduce", ALU.max, replica_groups=[list(range(n_cores))],
        ins=[cc_in_b.opt()], outs=[cc_out_b.opt()]),
         reads=["ccb"], writes=["ccbo"])
    k.dma(GP, wstat[0:1, 5:6], cc_out_b, reads=["ccbo"], writes=["ws5"])
    k.op(V, lambda e: e.reciprocal(wstat[0:1, 6:7], wstat[0:1, 5:6]),
         reads=["ws5"], writes=["ws6"])
    k.op(V, lambda e: e.tensor_scalar(wstat[0:1, 6:7], wstat[0:1, 6:7], 32767.0, None,
                                      ALU.mult),
         reads=["ws6"], writes=["ws6"])
    k.op(V, lambda e: e.reciprocal(wstat[0:1, 7:8], wstat[0:1, 6:7]),
         reads=["ws6"], writes=["ws7"])
    bc_scalar(wstat[0:1, 6:7], "ws6", 4)
    bc_scalar(wstat[0:1, 7:8], "ws7", 5)

    for c in range(32):
        w = wsm[c % 2]
        wk = f"wsm{c % 2}"
        k.dma(SY, w, wo0_d[ts(c, 128), :], writes=[wk])
        k.op(V, lambda e, w=w: e.reduce_max(wstat[:, 0:1], w, axis=AX.X,
                                            apply_absolute_value=True),
             reads=[wk], writes=["ws0"])
        k.op(V, lambda e: e.reciprocal(wstat[:, 1:2], wstat[:, 0:1]),
             reads=["ws0"], writes=["ws1"])
        k.op(V, lambda e: e.tensor_scalar(wstat[:, 1:2], wstat[:, 1:2], 127.0, None,
                                          ALU.mult),
             reads=["ws1"], writes=["ws1"])
        k.op(V, lambda e, w=w: e.tensor_scalar(w, w, wstat[:, 1:2], None, ALU.mult),
             reads=[wk, "ws1"], writes=[wk])
        emit_round(w, stg[0], wk, "stg0")
        k.op(V, lambda e, w=w: e.tensor_scalar(w, w, 127.0, -128.0, ALU.min, ALU.max),
             reads=[wk], writes=[wk])
        k.op(V, lambda e: e.tensor_scalar(wstat[:, 0:1], wstat[:, 0:1], scal[:, 5:6],
                                          1.0 / 127.0, ALU.mult, ALU.mult),
             reads=["ws0", "scal5"], writes=["ws0"])
        k.op(V, lambda e, w=w: e.tensor_scalar(w, w, wstat[:, 0:1], None, ALU.mult),
             reads=[wk, "ws0"], writes=[wk])
        k.dma(SY, wpsp[ts(c, 128), :], w, reads=[wk], writes=["wpsp"])

    gbank = ([psA[:, ts(i, 512)] for i in range(4)]
             + [psB[:, ts(i, 512)] for i in range(2)] + [psC, psD])
    gbkey = PSA + PSB + ["psC", "psD"]
    k.fence(["A2"], ["bo0bc"])
    k.dma(SY, bo0bc_s, bo0bc_d, writes=["bo0bc"])
    for grp in range(NTG):
        for c in range(32):
            h = stg[c % 2]
            hk = f"stg{c % 2}"
            k.dma(SY, h[:, 0:512],
                  bass.AP(hidsp.tensor, 128 * c * T + 512 * grp, [[T, 128], [1, 512]]),
                  reads=["hidsp"], writes=[hk])
            k.op(V, lambda e, h=h: e.tensor_scalar(h[:, 0:512], h[:, 0:512],
                                                   scal[:, 4:5], None, ALU.mult),
                 reads=[hk, "scal4"], writes=[hk])
            emit_round(h[:, 0:512], tile_c[:, 0:512], hk, "tile_c")
            k.op(V, lambda e, h=h: e.tensor_scalar(h[:, 0:512], h[:, 0:512],
                                                   32767.0, -32768.0, ALU.min,
                                                   ALU.max),
                 reads=[hk], writes=[hk])
            wp = wsm[c % 2]
            wpk = f"wsm{c % 2}"
            k.dma(SY, wp, wpsp[ts(c, 128), :], reads=["wpsp"], writes=[wpk])
            for sub in range(4):
                for half in range(2):
                    bi = sub * 2 + half
                    k.op(TE, lambda e, h=h, wp=wp, sub=sub, half=half, c=c, bi=bi:
                         e.matmul(gbank[bi], h[:, ts(sub, 128)], wp[:, ts(half, 512)],
                                  start=(c == 0), stop=(c == 31)),
                         reads=[hk, wpk], writes=[gbkey[bi]])
        for sub in range(4):
            tt = grp * 4 + sub
            for half in range(2):
                bi = sub * 2 + half
                k.op(V, lambda e, bi=bi, half=half: e.tensor_add(
                    tile_d[:, ts(half, 512)], gbank[bi], bo0bc_s[:, ts(half, 512)]),
                     reads=[gbkey[bi], "bo0bc"], writes=["tile_d"])
            k.op(S, lambda e, tt=tt: e.activation(A1[:, ts(tt, 1024)], tile_d,
                                                  AF.Silu),
                 reads=["tile_d"], writes=["A1"])

    k.fence(["A2", "bo0bc"], ["ktl0", "vl0", "cosq", "sinq", "qlat", "kvlat",
                              "wqb", "wkvb"])
    k.dma(SY, qlat2, qlatsp[0:64, :], reads=["qlatsp"], writes=["qlat"])
    k.dma(SY, kvlat2, qlatsp[64:96, :], reads=["qlatsp"], writes=["kvlat"])
    k.dma(SY, wqb_s, wqb_d, writes=["wqb"])
    k.dma(SY, wkvb_s, wkvb_d, writes=["wkvb"])
    k.op(GP, lambda e: e.iota(srow[0:1, 0:32], [[1, 32]], channel_multiplier=0,
                              allow_small_or_imprecise_dtypes=True),
         writes=["srow"])
    k.op(GP, lambda e: e.memset(srow[0:1, 40:41], 10000.0), writes=["sr40"])
    k.op(S, lambda e: e.activation(srow[0:1, 41:42], srow[0:1, 40:41], AF.Ln),
         reads=["sr40"], writes=["sr41"])
    k.op(V, lambda e: e.tensor_scalar(srow[0:1, 0:32], srow[0:1, 0:32],
                                      srow[0:1, 41:42], -1.0 / 32.0, ALU.mult,
                                      ALU.mult),
         reads=["srow", "sr41"], writes=["srow"])
    k.op(S, lambda e: e.activation(srow[0:1, 0:32], srow[0:1, 0:32], AF.Exp),
         reads=["srow"], writes=["srow"])
    k.op(TE, lambda e: e.matmul(psD[:, 0:32], ones1, srow[0:1, 0:32], start=True,
                                stop=True),
         reads=["ones1", "srow"], writes=["psD"])
    t_copy(V, invf, psD[:, 0:32], ["psD"], ["invf"])
    for t in range(NT):
        ang = rsc[:, 0:32]
        red = rsc[:, 32:64]
        k.op(GP, lambda e, t=t: e.iota(tcol, [[0, 1]], base=128 * t,
                                       channel_multiplier=1,
                                       allow_small_or_imprecise_dtypes=True),
             writes=["tcol"])
        k.op(V, lambda e: e.tensor_scalar(ang, invf, tcol[:, 0:1], None, ALU.mult),
             reads=["invf", "tcol"], writes=["rsc0"])
        k.op(V, lambda e: e.tensor_scalar(tile_c[:, 0:32], ang, 0.15915494309189535,
                                          None, ALU.mult),
             reads=["rsc0"], writes=["tile_c"])
        emit_round(tile_c[:, 0:32], tile_d[:, 0:32], "tile_c", "tile_d")
        k.op(V, lambda e: e.scalar_tensor_tensor(red, tile_c[:, 0:32], -TPH, ang,
                                                 ALU.mult, ALU.add),
             reads=["tile_c", "rsc0"], writes=["rsc1"])
        k.op(V, lambda e: e.scalar_tensor_tensor(red, tile_c[:, 0:32], -TPL, red,
                                                 ALU.mult, ALU.add),
             reads=["tile_c", "rsc1"], writes=["rsc1"])
        k.op(V, lambda e: e.tensor_scalar(red, red, PICLIP, -PICLIP, ALU.min,
                                          ALU.max),
             reads=["rsc1"], writes=["rsc1"])
        k.op(V, lambda e: e.tensor_scalar(ang, red, PHH, PHL, ALU.add, ALU.add),
             reads=["rsc1"], writes=["rsc0"])
        k.op(V, lambda e: e.tensor_scalar(tile_c[:, 0:32], ang, 3.1415927410125732,
                                          None, ALU.is_gt),
             reads=["rsc0"], writes=["tile_c"])
        k.op(V, lambda e: e.scalar_tensor_tensor(ang, tile_c[:, 0:32], -TPH, ang,
                                                 ALU.mult, ALU.add),
             reads=["tile_c", "rsc0"], writes=["rsc0"])
        k.op(V, lambda e: e.scalar_tensor_tensor(ang, tile_c[:, 0:32], -TPL, ang,
                                                 ALU.mult, ALU.add),
             reads=["tile_c", "rsc0"], writes=["rsc0"])
        k.op(V, lambda e: e.tensor_scalar(ang, ang, PICLIP, -PICLIP, ALU.min,
                                          ALU.max),
             reads=["rsc0"], writes=["rsc0"])
        k.op(S, lambda e, t=t: e.activation(cosq[:, ts(t, 32)], ang, AF.Sin),
             reads=["rsc0"], writes=["cosq"])
        k.op(S, lambda e, t=t: e.activation(sinq[:, ts(t, 32)], red, AF.Sin),
             reads=["rsc1"], writes=["sinq"])

    def emit_rope(src, src_key, dst, dst_key, t, nh):
        cB = bass.AP(cosq.tensor, cosq.offset + 32 * t,
                     [[cosq.ap[0][0], 128], [0, nh], [1, 32]])
        sB = bass.AP(sinq.tensor, sinq.offset + 32 * t,
                     [[sinq.ap[0][0], 128], [0, nh], [1, 32]])

        def sl(buf, off):
            return bass.AP(buf.tensor, buf.offset + off,
                           [[buf.ap[0][0], 128], [64, nh], [2, 32]])
        se, so = sl(src, 0), sl(src, 1)
        de, do = sl(dst, 0), sl(dst, 1)
        scr = bass.AP(tile_d.tensor, tile_d.offset,
                      [[tile_d.ap[0][0], 128], [32, nh], [1, 32]])
        k.op(V, lambda e: e.tensor_mul(de, se, cB), reads=[src_key, "cosq"],
             writes=[dst_key])
        k.op(V, lambda e: e.tensor_mul(scr, so, sB), reads=[src_key, "sinq"],
             writes=["tile_d"])
        k.op(V, lambda e: e.tensor_sub(de, de, scr), reads=[dst_key, "tile_d"],
             writes=[dst_key])
        k.op(V, lambda e: e.tensor_mul(do, se, sB), reads=[src_key, "sinq"],
             writes=[dst_key])
        k.op(V, lambda e: e.tensor_mul(scr, so, cB), reads=[src_key, "cosq"],
             writes=["tile_d"])
        k.op(V, lambda e: e.tensor_add(do, do, scr), reads=[dst_key, "tile_d"],
             writes=[dst_key])

    def emit_attn_chunk(t, layer):
        nk = t + 1
        pipe = nk <= 8
        k.fence(["tile_c"], ["tc0", "tc1"])

        def cfg(h):
            if layer == 0:
                kvh = h // 4
                krow = ktl0[64 * (kvh % 2):64 * (kvh % 2) + 64, :]
                kcol0 = (kvh // 2) * T
                kkey = "ktl0"
                vb, vkey = vl0, "vl0"
                vcol = lambda b, kvh=kvh: 256 * b + 64 * kvh
                qc, qo = l0_chunk_off(h)
            else:
                krow = A2[64 * (h % 2):64 * (h % 2) + 64, :]
                kcol0 = (h // 2) * T
                kkey = "A2"
                vb, vkey = A1, "A1"
                vcol = lambda b, h=h: 1024 * b + 64 * h
                qc, qo = h // 2, 64 * (h % 2)
            lh = qTt[qo:qo + 64, ts(qc, 128)]
            par = h % 2 if pipe else 0
            base = 1024 * par
            pskeys = [PSA[2 * par], PSA[2 * par + 1]] if pipe else PSA
            return krow, kcol0, kkey, vb, vkey, vcol, lh, par, base, pskeys

        def emit_qk(h):
            krow, kcol0, kkey, _, _, _, lh, par, base, pskeys = cfg(h)
            nfull = (128 * nk) // 512
            rem = 128 * nk - 512 * nfull

            def qk_span(e, lh, krow, kcol0, k0, k1, remw, base):
                ins = None
                for kt_ in range(k0, k1):
                    ins = e.matmul(
                        psA[:, base + 512 * kt_: base + 512 * (kt_ + 1)], lh,
                        krow[:, kcol0 + 512 * kt_: kcol0 + 512 * (kt_ + 1)],
                        start=True, stop=True)
                if remw:
                    ins = e.matmul(
                        psA[:, base + 512 * k1: base + 512 * k1 + remw], lh,
                        krow[:, kcol0 + 512 * k1: kcol0 + 512 * k1 + remw],
                        start=True, stop=True)
                return ins
            if pipe:
                k.op(TE, lambda e, lh=lh, krow=krow, kcol0=kcol0: qk_span(
                    e, lh, krow, kcol0, 0, nfull, rem, base),
                     reads=["qTt", kkey], writes=pskeys)
            else:
                k.op(TE, lambda e, lh=lh, krow=krow, kcol0=kcol0: qk_span(
                    e, lh, krow, kcol0, 0, 2, 0, 0),
                     reads=["qTt", kkey], writes=["psA0", "psA1"])
                k.op(TE, lambda e, lh=lh, krow=krow, kcol0=kcol0: qk_span(
                    e, lh, krow, kcol0, 2, nfull, rem, 0),
                     reads=["qTt", kkey], writes=["psA2", "psA3"])

        def emit_softmax(h):
            _, _, _, _, _, _, _, par, base, pskeys = cfg(h)
            tc = tile_c[:, 128 * par: 128 * par + 128]
            tck = f"tc{par}"
            scol = 8 * par

            def st(i):
                return stats[:, scol + i: scol + i + 1]

            def stk(i):
                return f"st{i}" if par == 0 else f"sho{i}"
            if pipe:
                pst = pstrip[:, 1024 * par: 1024 * par + 1024]
                psks = [f"pss{par}"]
                k.op(V, lambda e, tc=tc, base=base: e.tensor_add(
                    tc, psA[:, base + 128 * t: base + 128 * (t + 1)], maskd),
                     reads=pskeys + ["maskd"], writes=[tck])
                if t > 0:
                    k.op(V, lambda e, base=base: e.reduce_max(
                        st(0), psA[:, base: base + 128 * t], axis=AX.X),
                         reads=pskeys, writes=[stk(0)])
                    k.op(V, lambda e, tc=tc: e.reduce_max(st(1), tc, axis=AX.X),
                         reads=[tck], writes=[stk(1)])
                    k.op(V, lambda e: e.tensor_max(st(0), st(0), st(1)),
                         reads=[stk(0), stk(1)], writes=[stk(0)])
                else:
                    k.op(V, lambda e, tc=tc: e.reduce_max(st(0), tc, axis=AX.X),
                         reads=[tck], writes=[stk(0)])
                k.op(V, lambda e: e.tensor_scalar(st(2), st(0), -0.125, None,
                                                  ALU.mult),
                     reads=[stk(0)], writes=[stk(2)])
                if t > 0:
                    k.op(S, lambda e, pst=pst, base=base: e.activation(
                        pst[:, 0:128 * t], psA[:, base: base + 128 * t], AF.Exp,
                        bias=st(2), scale=0.125, accum_out=st(3)),
                         reads=pskeys + [stk(2)], writes=psks + [stk(3)])
                k.op(S, lambda e, pst=pst, tc=tc: e.activation(
                    pst[:, 128 * t:128 * (t + 1)], tc, AF.Exp,
                    bias=st(2), scale=0.125, accum_out=st(4)),
                     reads=[tck, stk(2)], writes=psks + [stk(4)])
                if t > 0:
                    k.op(V, lambda e: e.tensor_add(st(5), st(3), st(4)),
                         reads=[stk(3), stk(4)], writes=[stk(5)])
                else:
                    t_copy(V, st(5), st(4), [stk(4)], [stk(5)])
            else:
                hw = 128 * t - 1024
                k.op(V, lambda e, tc=tc: e.tensor_add(
                    tc, psA[:, 128 * t: 128 * (t + 1)], maskd),
                     reads=["psA2", "psA3", "maskd"], writes=[tck])
                k.op(V, lambda e: e.reduce_max(st(0), psA[:, 0:1024], axis=AX.X),
                     reads=["psA0", "psA1"], writes=[stk(0)])
                if hw:
                    k.op(V, lambda e, hw=hw: e.reduce_max(
                        st(1), psA[:, 1024:1024 + hw], axis=AX.X),
                         reads=["psA2", "psA3"], writes=[stk(1)])
                    k.op(V, lambda e: e.tensor_max(st(0), st(0), st(1)),
                         reads=[stk(0), stk(1)], writes=[stk(0)])
                k.op(V, lambda e, tc=tc: e.reduce_max(st(1), tc, axis=AX.X),
                     reads=[tck], writes=[stk(1)])
                k.op(V, lambda e: e.tensor_max(st(0), st(0), st(1)),
                     reads=[stk(0), stk(1)], writes=[stk(0)])
                k.op(V, lambda e: e.tensor_scalar(st(2), st(0), -0.125, None,
                                                  ALU.mult),
                     reads=[stk(0)], writes=[stk(2)])
                k.op(S, lambda e: e.activation(
                    pstrip[:, 0:1024], psA[:, 0:1024], AF.Exp,
                    bias=st(2), scale=0.125, accum_out=st(3)),
                     reads=["psA0", "psA1", stk(2)], writes=["pss0", stk(3)])
                if hw:
                    k.op(S, lambda e, hw=hw: e.activation(
                        pstrip[:, 1024:1024 + hw], psA[:, 1024:1024 + hw],
                        AF.Exp, bias=st(2), scale=0.125, accum_out=st(6)),
                         reads=["psA2", "psA3", stk(2)],
                         writes=["pss1", stk(6)])
                k.op(S, lambda e, tc=tc: e.activation(
                    pstrip[:, 128 * t:128 * (t + 1)], tc, AF.Exp,
                    bias=st(2), scale=0.125, accum_out=st(4)),
                     reads=[tck, stk(2)], writes=["pss1", stk(4)])
                k.op(V, lambda e: e.tensor_add(st(5), st(3), st(4)),
                     reads=[stk(3), stk(4)], writes=[stk(5)])
                if hw:
                    k.op(V, lambda e: e.tensor_add(st(5), st(5), st(6)),
                         reads=[stk(5), stk(6)], writes=[stk(5)])
            k.op(V, lambda e, h=h: e.reciprocal(recips[:, h:h + 1], st(5)),
                 reads=[stk(5)], writes=["recips"])

        def emit_tp_av(h):
            _, _, _, vb, vkey, vcol, _, par, base, pskeys = cfg(h)
            pst = pstrip[:, 1024 * par: 1024 * par + 1024] if pipe else pstrip
            psks = [f"pss{par}"] if pipe else ["pss0", "pss1"]
            ptt = ptstrip[:, 1024 * par: 1024 * par + 1024] if pipe else ptstrip
            ptks = [f"pts{par}"] if pipe else ["pts0", "pts1"]
            nb4 = (nk + 3) // 4
            for b4 in range(nb4):
                nblk = min(4, nk - 4 * b4)

                def tp(e, b4=b4, nblk=nblk, pst=pst):
                    ins = None
                    for i in range(nblk):
                        ins = e.transpose(psC[:, ts(i, 128)],
                                          pst[:, ts(4 * b4 + i, 128)], ident)
                    return ins
                k.op(TE, tp, reads=psks + ["ident"], writes=["psC"])
                k.op(V, lambda e, b4=b4, nblk=nblk, ptt=ptt: e.tensor_copy(
                    ptt[:, 512 * b4: 512 * b4 + 128 * nblk], psC[:, 0:128 * nblk]),
                     reads=["psC"], writes=ptks)

            def av(e, h=h, nk=nk, vcol=vcol, vb=vb, ptt=ptt):
                ins = None
                for b in range(nk):
                    ins = e.matmul(psB[:, 64 * h: 64 * h + 64], ptt[:, ts(b, 128)],
                                   vb[:, vcol(b): vcol(b) + 64],
                                   start=(b == 0), stop=(b == nk - 1))
                return ins
            k.op(TE, av, reads=ptks + [vkey], writes=[PSB[h // 8]])
            if h % 8 == 7:
                for hh in range(h - 7, h + 1):
                    k.op(V, lambda e, hh=hh: e.tensor_scalar(
                        tile_a[:, 64 * hh: 64 * hh + 64],
                        psB[:, 64 * hh: 64 * hh + 64], recips[:, hh:hh + 1], None,
                        ALU.mult),
                         reads=[PSB[h // 8], "recips"], writes=["tile_a"])

        emit_qk(0)
        for h in range(NH):
            emit_softmax(h)
            if h + 1 < NH:
                emit_qk(h + 1)
            emit_tp_av(h)
        k.fence(["tc0", "tc1"], ["tile_c"])

    k.fence(["st8", "st9", "st10", "st11", "st12", "st13"],
            ["sho0", "sho1", "sho2", "sho3", "sho4", "sho5"])
    for t in range(NT):
        k.op(TE, lambda e, t=t: e.matmul(psD, kvlat2[:, ts(t, 128)], wkvb_s,
                                         start=True, stop=True),
             reads=["kvlat", "wkvb"], writes=["psD"])
        t_copy(V, tile_b[:, 0:512], psD, ["psD"], ["tile_b"])
        k.op(V, lambda e, t=t: e.tensor_copy(
            bass.AP(vl0.tensor, vl0.offset + 256 * t,
                    [[vl0.ap[0][0], 128], [64, 4], [1, 64]]),
            bass.AP(tile_b.tensor, tile_b.offset + 1,
                    [[tile_b.ap[0][0], 128], [128, 4], [2, 64]])),
             reads=["tile_b"], writes=["vl0"])
        cB = bass.AP(cosq.tensor, cosq.offset + 32 * t,
                     [[cosq.ap[0][0], 128], [0, 4], [1, 32]])
        sB = bass.AP(sinq.tensor, sinq.offset + 32 * t,
                     [[sinq.ap[0][0], 128], [0, 4], [1, 32]])
        ke = bass.AP(tile_b.tensor, tile_b.offset,
                     [[tile_b.ap[0][0], 128], [128, 4], [4, 32]])
        ko_ = bass.AP(tile_b.tensor, tile_b.offset + 2,
                      [[tile_b.ap[0][0], 128], [128, 4], [4, 32]])
        re_ = bass.AP(tile_c.tensor, tile_c.offset,
                      [[tile_c.ap[0][0], 128], [128, 4], [4, 32]])
        ro_ = bass.AP(tile_c.tensor, tile_c.offset + 2,
                      [[tile_c.ap[0][0], 128], [128, 4], [4, 32]])
        scrA = bass.AP(tile_d.tensor, tile_d.offset,
                       [[tile_d.ap[0][0], 128], [32, 4], [1, 32]])
        k.op(V, lambda e: e.tensor_mul(re_, ke, cB), reads=["tile_b", "cosq"],
             writes=["tile_c"])
        k.op(V, lambda e: e.tensor_mul(scrA, ko_, sB), reads=["tile_b", "sinq"],
             writes=["tile_d"])
        k.op(V, lambda e: e.tensor_sub(re_, re_, scrA), reads=["tile_c", "tile_d"],
             writes=["tile_c"])
        k.op(V, lambda e: e.tensor_mul(ro_, ke, sB), reads=["tile_b", "sinq"],
             writes=["tile_c"])
        k.op(V, lambda e: e.tensor_mul(scrA, ko_, cB), reads=["tile_b", "cosq"],
             writes=["tile_d"])
        k.op(V, lambda e: e.tensor_add(ro_, ro_, scrA), reads=["tile_c", "tile_d"],
             writes=["tile_c"])
        for kc in range(2):
            kslice = bass.AP(tile_c.tensor, tile_c.offset + 256 * kc,
                             [[tile_c.ap[0][0], 128], [2, 128]])
            k.op(TE, lambda e, kslice=kslice: e.transpose(psC[:, 0:128], kslice,
                                                          ident),
                 reads=["tile_c", "ident"], writes=["psC"])
            k.op(V, lambda e, kc=kc, t=t: e.tensor_copy(
                ktl0[:, kc * T + 128 * t: kc * T + 128 * t + 128], psC[:, 0:128]),
                 reads=["psC"], writes=["ktl0"])
        for half in range(2):
            k.op(TE, lambda e, t=t, half=half: e.matmul(
                psB[:, ts(half, 512)], qlat2[:, ts(t, 128)],
                wqb_s[:, ts(half, 512)], start=True, stop=True),
                 reads=["qlat", "wqb"], writes=[PSB[half]])
        for half in range(2):
            k.op(V, lambda e, half=half: e.tensor_copy(tile_b[:, ts(half, 512)],
                                                       psB[:, ts(half, 512)]),
                 reads=[PSB[half]], writes=["tile_b"])
        emit_rope(tile_b, "tile_b", tile_c, "tile_c", t, NH)
        emit_transpose_tile(tile_c, "tile_c", qTt, "qTt")
        emit_attn_chunk(t, 0)
        emit_transpose_tile(tile_a, "tile_a", tile_b, "tile_b")
        for mc in range(8):
            w = wsm[mc % 2]
            wk = f"wsm{mc % 2}"
            k.dma(SY, w, bass.AP(wout0_d.tensor, 128 * mc,
                                 [[D, 128], [128 * D, 8], [1, 128]]),
                  writes=[wk])
            pairs = [(w[:, ts(c, 128)], tile_b[:, ts(c, 128)]) for c in range(8)]
            ps, pk = (psC, "psC") if mc % 2 == 0 else (psD, "psD")
            mm_group(ps[:, 0:128], pairs, pk, [wk, "tile_b"])
            k.op(V, lambda e, mc=mc, ps=ps: e.tensor_scalar(
                tile_c[:, ts(mc, 128)], ps[:, 0:128], bout0_s[:, mc:mc + 1], None,
                ALU.add),
                 reads=[pk, "bout0"], writes=["tile_c"])
        emit_transpose_tile(tile_c, "tile_c", tile_d, "tile_d")
        k.dma(SY, tile_a, x_d[ts(t, 128), :], writes=["tile_a"])
        k.op(V, lambda e, t=t: e.tensor_mul(tile_d, tile_d, A1[:, ts(t, 1024)]),
             reads=["tile_d", "A1"], writes=["tile_d"])
        k.op(V, lambda e: e.tensor_add(tile_d, tile_d, tile_a),
             reads=["tile_d", "tile_a"], writes=["tile_d"])
        k.dma(SY, x1sp[ts(t, 128), :], tile_d, reads=["tile_d"], writes=["x1sp"])
        k.op(V, lambda e: e.reduce_sum(stats[:, 0:1], tile_d, axis=AX.X),
             reads=["tile_d"], writes=["st0"])
        k.op(S, lambda e: e.activation(tile_c, tile_d, AF.Square,
                                       accum_out=stats[:, 1:2]),
             reads=["tile_d"], writes=["tile_c", "st1"])
        k.op(V, lambda e, t=t: e.tensor_scalar(lnm[:, 2 * t:2 * t + 1], stats[:, 0:1],
                                               1.0 / D, None, ALU.mult),
             reads=["st0"], writes=["lnm"])
        k.op(V, lambda e, t=t: e.tensor_scalar(stats[:, 4:5], lnm[:, 2 * t:2 * t + 1],
                                               lnm[:, 2 * t:2 * t + 1], None,
                                               ALU.mult),
             reads=["lnm"], writes=["st4"])
        k.op(V, lambda e: e.tensor_scalar(stats[:, 5:6], stats[:, 1:2], 1.0 / D, 1e-5,
                                          ALU.mult, ALU.add),
             reads=["st1"], writes=["st5"])
        k.op(V, lambda e: e.tensor_sub(stats[:, 5:6], stats[:, 5:6], stats[:, 4:5]),
             reads=["st5", "st4"], writes=["st5"])
        k.op(V, lambda e: e.reciprocal(stats[:, 6:7], stats[:, 5:6]),
             reads=["st5"], writes=["st6"])
        k.op(S, lambda e, t=t: e.activation(lnm[:, 2 * t + 1:2 * t + 2],
                                            stats[:, 6:7], AF.Sqrt),
             reads=["st6"], writes=["lnm"])
        k.op(V, lambda e, t=t: e.tensor_scalar(tile_a, tile_d,
                                               lnm[:, 2 * t:2 * t + 1],
                                               lnm[:, 2 * t + 1:2 * t + 2],
                                               ALU.subtract, ALU.mult),
             reads=["tile_d", "lnm"], writes=["tile_a"])
        emit_transpose_tile(tile_a, "tile_a", tile_c, "tile_c")
        k.dma(SY, xn1sp[ts(t, 128), :], tile_c, reads=["tile_c"], writes=["xn1sp"])

    k.fence(["ktl0", "vl0", "cosq", "sinq", "qlat", "kvlat", "wqb", "wkvb"],
            ["A2"])
    vbank = ([psA[:, ts(i, 512)] for i in range(4)]
             + [psB[:, ts(i, 512)] for i in range(2)] + [psC, psD])
    vbkey = PSA + PSB + ["psC", "psD"]
    for tsl in range(NTG):
        for i in range(4):
            k.dma(SY, tiles[:, ts(i, 1024)], xn1sp[ts(4 * tsl + i, 128), :],
                  reads=["xn1sp"], writes=[TILEK[i]])
        for mc in range(8):
            w = wsm[mc % 2]
            wk = f"wsm{mc % 2}"
            k.dma(SY, w, bass.AP(wkT_d.tensor, 128 * mc,
                                 [[D, 128], [128 * D, 8], [1, 128]]),
                  writes=[wk])
            ps, pk = (psC, "psC") if mc % 2 == 0 else (psD, "psD")
            pairs = [(w[:, ts(c, 128)], xnT_ap(tiles, c, 0, 512)) for c in range(8)]
            mm_group(ps, pairs, pk, [wk] + TILEK)
            k.op(V, lambda e, mc=mc, tsl=tsl, ps=ps: e.tensor_copy(
                A2[:, mc * T + 512 * tsl: mc * T + 512 * tsl + 512], ps),
                 reads=[pk], writes=["A2"])
        for half in range(2):
            for c in range(8):
                w = wsm[c % 2]
                wk = f"wsm{c % 2}"
                k.dma(SY, w[:, 0:512], wvT_d[ts(c, 128), ts(half, 512)], writes=[wk])
                for i in range(4):
                    bi = 2 * i + half
                    k.op(TE, lambda e, w=w, c=c, i=i, bi=bi: e.matmul(
                        vbank[bi], tiles[:, 1024 * i + 128 * c: 1024 * i + 128 * c + 128],
                        w[:, 0:512], start=(c == 0), stop=(c == 7)),
                         reads=[wk, TILEK[i]], writes=[vbkey[bi]])
        for i in range(4):
            for half in range(2):
                bi = 2 * i + half
                k.op(V, lambda e, tsl=tsl, i=i, half=half, bi=bi: e.tensor_copy(
                    A1[:, 1024 * (4 * tsl + i) + 512 * half:
                       1024 * (4 * tsl + i) + 512 * half + 512], vbank[bi]),
                     reads=[vbkey[bi]], writes=["A1"])

    for t in range(NT):
        k.dma(SY, stg[0], xn1sp[ts(t, 128), :], reads=["xn1sp"], writes=["stg0"])
        for mc in range(8):
            w = wsm[mc % 2]
            wk = f"wsm{mc % 2}"
            k.dma(SY, w, bass.AP(wqT_d.tensor, 128 * mc,
                                 [[D, 128], [128 * D, 8], [1, 128]]),
                  writes=[wk])
            ps, pk = (psC, "psC") if mc % 2 == 0 else (psD, "psD")
            pairs = [(w[:, ts(c, 128)], stg[0][:, ts(c, 128)]) for c in range(8)]
            mm_group(ps[:, 0:128], pairs, pk, [wk, "stg0"])
            k.op(V, lambda e, mc=mc, ps=ps: e.tensor_copy(qTt[:, ts(mc, 128)],
                                                          ps[:, 0:128]),
                 reads=[pk], writes=["qTt"])
        emit_attn_chunk(t, 1)
        emit_transpose_tile(tile_a, "tile_a", tile_b, "tile_b")
        for mc in range(8):
            w = wsm[mc % 2]
            wk = f"wsm{mc % 2}"
            k.dma(SY, w, bass.AP(woT_d.tensor, 128 * mc,
                                 [[D, 128], [128 * D, 8], [1, 128]]),
                  writes=[wk])
            pairs = [(w[:, ts(c, 128)], tile_b[:, ts(c, 128)]) for c in range(8)]
            ps, pk = (psC, "psC") if mc % 2 == 0 else (psD, "psD")
            mm_group(ps[:, 0:128], pairs, pk, [wk, "tile_b"])
            k.op(V, lambda e, mc=mc, ps=ps: e.tensor_copy(tile_c[:, ts(mc, 128)],
                                                          ps[:, 0:128]),
                 reads=[pk], writes=["tile_c"])
        emit_transpose_tile(tile_c, "tile_c", tile_d, "tile_d")
        k.dma(SY, ssp[ts(t, 128), :], tile_d, reads=["tile_d"], writes=["ssp"])

    k.fence(["A1"], P9C_A1)
    k.fence(["A2"], P9C_A2)
    decsl = [(p, off, w) for p in range(P_TREE)
             for off, w in ((0, 512), (512, 512), (1024, 340))]
    l5sl = [(p, 1364 + 512 * j, 512) for p in range(P_TREE) for j in range(8)]
    decck = [(p, o, w) for p in range(P_TREE)
             for o, w in [(128 * i, 128) for i in range(10)] + [(1280, 84)]]
    NCK = len(decck) + len(l5sl) * 4

    for t in range(NT):
        k.dma(SY, xn1t, xn1sp[ts(t, 128), :], reads=["xn1sp"], writes=["xn1t"])
        for si, (p, off, w) in enumerate(decsl):
            hid0 = p * PBLK + off
            wb = wbufA[si % 2]
            wbk = f"wbufA{si % 2}"
            k.dma(SY, bass.AP(wb.tensor, wb.offset, [[wb.ap[0][0], 128], [w, 8], [1, w]]),
                  bass.AP(w1T_d.tensor, hid0, [[HID1, 128], [128 * HID1, 8], [1, w]]),
                  writes=[wbk])
            bs = bslc[si % 2]
            bsk = f"bslc{si % 2}"
            k.dma(SY, bs[:, 0:w], b1bc_d[:, hid0:hid0 + w], writes=[bsk])
            ps, pk = (psC, "psC") if si % 2 == 0 else (psD, "psD")
            pairs = [(xn1t[:, ts(c, 128)], wb[:, c * w:(c + 1) * w]) for c in range(8)]
            mm_group(ps[:, 0:w], pairs, pk, ["xn1t", wbk])
            asb = aslc[si % 2]
            ask = f"aslc{si % 2}"
            k.op(V, lambda e, ps=ps, bs=bs, w=w, asb=asb: e.tensor_add(
                asb[:, 0:w], ps[:, 0:w], bs[:, 0:w]),
                 reads=[pk, bsk], writes=[ask])
            k.op(S, lambda e, p=p, off=off, w=w, asb=asb: e.activation(
                actdec[:, p * DEC_W + off: p * DEC_W + off + w], asb[:, 0:w],
                AF.Gelu),
                 reads=[ask], writes=["actdec"])
        AD = actdec

        def A_f(f):
            return bass.AP(AD.tensor, AD.offset + f,
                           [[AD.ap[0][0], 128], [DEC_W, 4], [4, DEC_G]])
        k.op(V, lambda e: e.tensor_max(trm, A_f(0), A_f(1)),
             reads=["actdec"], writes=["trm"])
        k.op(V, lambda e: e.tensor_max(trm, trm, A_f(2)),
             reads=["trm", "actdec"], writes=["trm"])
        k.op(V, lambda e: e.tensor_max(trm, trm, A_f(3)),
             reads=["trm", "actdec"], writes=["trm"])
        for f in range(4):
            k.op(V, lambda e, f=f: e.tensor_tensor(tre[f], A_f(f), trm, ALU.is_equal),
                 reads=["actdec", "trm"], writes=[f"tre{f}"])
        k.op(V, lambda e: e.tensor_scalar(trc, tre[0], -1.0, 1.0, ALU.mult, ALU.add),
             reads=["tre0"], writes=["trc"])
        for f in range(1, 4):
            k.op(V, lambda e, f=f: e.tensor_mul(trt, tre[f], trc),
                 reads=[f"tre{f}", "trc"], writes=["trt"])
            if f < 3:
                k.op(V, lambda e, f=f: e.tensor_scalar(tre[f], tre[f], -1.0, 1.0,
                                                       ALU.mult, ALU.add),
                     reads=[f"tre{f}"], writes=[f"tre{f}"])
                k.op(V, lambda e, f=f: e.tensor_mul(trc, trc, tre[f]),
                     reads=["trc", f"tre{f}"], writes=["trc"])
            t_copy(V, tre[f], trt, ["trt"], [f"tre{f}"])
        M = mask1h
        k.op(V, lambda e: e.memset(
            bass.AP(M.tensor, M.offset, [[M.ap[0][0], 128], [G_TREE, 4], [1, 1]]),
            1.0),
             writes=["mask1h"])
        plat = 0
        for d in range(5):
            sz = 4 ** d
            for j in range(4):
                wprev = bass.AP(M.tensor, M.offset + plat,
                                [[M.ap[0][0], 128], [G_TREE, 4], [1, sz]])
                hj = bass.AP(treb.tensor, treb.offset + j * (P_TREE * DEC_G) + plat,
                             [[treb.ap[0][0], 128], [DEC_G, 4], [1, sz]])
                out = bass.AP(M.tensor, M.offset + plat + sz + j,
                              [[M.ap[0][0], 128], [G_TREE, 4], [4, sz]])
                k.op(V, lambda e, out=out, wprev=wprev, hj=hj: e.tensor_tensor(
                    out, wprev, hj, ALU.mult),
                     reads=["mask1h", f"tre{j}"], writes=["mask1h"])
            plat += sz
        mdec = bass.AP(M.tensor, M.offset,
                       [[M.ap[0][0], 128], [G_TREE, 4], [1, DEC_G], [0, 4]])
        adec4 = bass.AP(AD.tensor, AD.offset,
                        [[AD.ap[0][0], 128], [DEC_W, 4], [4, DEC_G], [1, 4]])
        k.op(V, lambda e: e.tensor_tensor(adec4, adec4, mdec, ALU.mult),
             reads=["actdec", "mask1h"], writes=["actdec"])
        nck = 0
        sib = 0
        for p, o, w in decck:
            k.op(TE, lambda e, p=p, o=o, w=w: e.transpose(
                psC[0:w, 0:128], actdec[:, p * DEC_W + o: p * DEC_W + o + w], ident),
                 reads=["actdec", "ident"], writes=["psC"])
            ms = mstg[sib % 2]
            msk = f"mstg{sib % 2}"
            k.op(V, lambda e, w=w, ms=ms: e.tensor_copy(ms[0:w, 0:128],
                                                        psC[0:w, 0:128]),
                 reads=["psC"], writes=[msk])
            wb = wbufB[sib % 2]
            wbk = f"wbufB{sib % 2}"
            hid0 = p * PBLK + o
            k.dma(SY, wb[0:w, 0:1024] if w == 128 else
                  bass.AP(wb.tensor, wb.offset, [[wb.ap[0][0], w], [1, 1024]]),
                  w2T_d[hid0:hid0 + w, :], writes=[wbk])
            for half in range(2):
                k.op(TE, lambda e, ms=ms, wb=wb, w=w, half=half, nck=nck: e.matmul(
                    psB[:, ts(half, 512)], ms[0:w, 0:128], wb[0:w, ts(half, 512)],
                    start=(nck == 0), stop=(nck == NCK - 1)),
                     reads=[msk, wbk], writes=[PSB[half]])
            nck += 1
            sib += 1
        for p, off, w in l5sl:
            hid0 = p * PBLK + off
            wb = wbufA[sib % 2]
            wbk = f"wbufA{sib % 2}"
            k.dma(SY, bass.AP(wb.tensor, wb.offset,
                              [[wb.ap[0][0], 128], [w, 8], [1, w]]),
                  bass.AP(w1T_d.tensor, hid0, [[HID1, 128], [128 * HID1, 8], [1, w]]),
                  writes=[wbk])
            bs = bslc[sib % 2]
            bsk = f"bslc{sib % 2}"
            k.dma(SY, bs[:, 0:w], b1bc_d[:, hid0:hid0 + w], writes=[bsk])
            ps, pk = (psC, "psC") if sib % 2 == 0 else (psD, "psD")
            pairs = [(xn1t[:, ts(c, 128)], wb[:, c * w:(c + 1) * w]) for c in range(8)]
            mm_group(ps[:, 0:w], pairs, pk, ["xn1t", wbk])
            asb = aslc[sib % 2]
            ask = f"aslc{sib % 2}"
            k.op(V, lambda e, ps=ps, bs=bs, w=w, asb=asb: e.tensor_add(
                asb[:, 0:w], ps[:, 0:w], bs[:, 0:w]),
                 reads=[pk, bsk], writes=[ask])
            k.op(S, lambda e, asb=asb, w=w: e.activation(asb[:, 0:w], asb[:, 0:w],
                                                         AF.Gelu),
                 reads=[ask], writes=[ask])
            g0 = DEC_G + (off - DEC_W) // 4
            mB = bass.AP(M.tensor, M.offset + p * G_TREE + g0,
                         [[M.ap[0][0], 128], [1, w // 4], [0, 4]])
            k.op(V, lambda e, asb=asb, mB=mB, w=w: e.tensor_tensor(
                asb[:, 0:w], asb[:, 0:w], mB, ALU.mult),
                 reads=[ask, "mask1h"], writes=[ask])
            for blk in range(4):
                k.op(TE, lambda e, asb=asb, blk=blk: e.transpose(
                    psA[:, ts(blk, 128)], asb[:, ts(blk, 128)], ident),
                     reads=[ask, "ident"], writes=["psA0"])
            ms = mstg[sib % 2]
            msk = f"mstg{sib % 2}"
            k.op(V, lambda e, ms=ms: e.tensor_copy(ms, psA[:, 0:512]),
                 reads=["psA0"], writes=[msk])
            wb2 = wbufB[sib % 2]
            wb2k = f"wbufB{sib % 2}"
            k.dma(SY, bass.AP(wb2.tensor, wb2.offset,
                              [[wb2.ap[0][0], 128], [1024, 4], [1, 1024]]),
                  bass.AP(w2T_d.tensor, hid0 * D, [[D, 128], [128 * D, 4], [1, 1024]]),
                  writes=[wb2k])
            for blk in range(4):
                for half in range(2):
                    k.op(TE, lambda e, ms=ms, wb2=wb2, blk=blk, half=half, nck=nck:
                         e.matmul(psB[:, ts(half, 512)], ms[:, ts(blk, 128)],
                                  wb2[:, 1024 * blk + 512 * half:
                                      1024 * blk + 512 * (half + 1)],
                                  start=(nck == 0), stop=(nck == NCK - 1)),
                         reads=[msk, wb2k], writes=[PSB[half]])
                nck += 1
            sib += 1
        assert nck == NCK
        for half in range(2):
            k.op(S, lambda e, half=half: e.activation(tile_c[:, ts(half, 512)],
                                                      psB[:, ts(half, 512)], AF.Silu),
                 reads=[PSB[half]], writes=["tile_c"])
        k.dma(SY, tile_a, x1sp[ts(t, 128), :], reads=["x1sp"], writes=["tile_a"])
        k.dma(SY, tile_b, ssp[ts(t, 128), :], reads=["ssp"], writes=["tile_b"])
        k.op(V, lambda e: e.tensor_mul(tile_c, tile_c, tile_b),
             reads=["tile_c", "tile_b"], writes=["tile_c"])
        k.op(V, lambda e: e.tensor_add(tile_c, tile_c, tile_a),
             reads=["tile_c", "tile_a"], writes=["tile_c"])
        k.dma(SY, out_d[ts(t, 128), :], tile_c, reads=["tile_c"], writes=["outd"])

    stack = k.build()
    return nc, stack


def prep_shared(inputs):
    def f32(a):
        return np.ascontiguousarray(np.asarray(a, dtype=np.float32))
    wqb = np.asarray(inputs["L0_attn_Wqb"])
    return dict(
        wqa=f32(inputs["L0_attn_Wqa"]),
        wqb=f32(wqb[:, l0_wqb_perm()]),
        wkva=f32(inputs["L0_attn_Wkva"]),
        wkvb=f32(inputs["L0_attn_Wkvb"]),
        wout0=f32(inputs["L0_attn_Wout"]),
        bout0=f32(np.asarray(inputs["L0_attn_bout"]).reshape(8, 128).T),
        winT0=f32(np.asarray(inputs["L0_ffn_W_in"]).T),
        bin0=f32(np.asarray(inputs["L0_ffn_b_in"]).reshape(32, 128).T),
        wo0=f32(inputs["L0_ffn_W_out"]),
        bo0bc=f32(np.broadcast_to(np.asarray(inputs["L0_ffn_b_out"],
                                             dtype=np.float32)[None, :], (128, D))),
        wqT=f32(np.asarray(inputs["L1_attn_WQ"]).T),
        wkT=f32(np.asarray(inputs["L1_attn_WK"]).T),
        wvT=f32(np.asarray(inputs["L1_attn_WV"]).T),
        woT=f32(np.asarray(inputs["L1_attn_WO"]).T),
        w1T=f32(np.asarray(inputs["L1_ffn_W_in"]).T),
        b1bc=f32(np.broadcast_to(np.asarray(inputs["L1_ffn_b_in"],
                                            dtype=np.float32)[None, :], (128, HID1))),
        w2T=f32(np.asarray(inputs["L1_ffn_W_out"]).T),
        ident=np.eye(128, dtype=np.float32),
    )


def kernel(**inputs):
    T = 2048
    n_cores = 8
    x = np.ascontiguousarray(np.asarray(inputs["L0_attn_x"], dtype=np.float32))
    shared = prep_shared(inputs)
    nc, stack = build(T=T, n_cores=n_cores)
    in_maps = [dict(shared, x=np.ascontiguousarray(x[c])) for c in range(n_cores)]
    res = run_bass_kernel_spmd(nc, in_maps, core_ids=list(range(n_cores)))
    stack.close()
    return np.stack([np.asarray(r["out"], dtype=np.float32) for r in res.results])


# version 12 (final)
# speedup vs baseline: 1.0211x; 1.0211x over previous
import contextlib

import numpy as np

import concourse.bass as bass
import concourse.mybir as mybir
from concourse.bass_utils import run_bass_kernel_spmd

F32 = mybir.dt.float32
BF16 = mybir.dt.bfloat16
AF = mybir.ActivationFunctionType
ALU = mybir.AluOpType
AX = mybir.AxisListType

ENGINES = ("tensor", "vector", "scalar", "sync", "gpsimd")
D = 1024
NH = 16
HD = 64
DFF0 = 4096
P_TREE = 4
G_TREE = 1365
DEC_G = 341
DEC_W = DEC_G * 4
HID1 = P_TREE * G_TREE * 4
PBLK = G_TREE * 4

C32 = 12582912.0
C64 = 6755399441055744.0
NEGBIG = -1.0e30
TPH = 6.2831854820251465
TPL = -1.7484556025237907e-07
PHH = 1.5707963705062866
PHL = -4.371139000186241e-08
PICLIP = 3.1415925


class _Op:
    __slots__ = ("eng", "fn", "reads", "writes", "dma", "deps", "inc")

    def __init__(self, eng, fn, reads, writes, dma):
        self.eng, self.fn = eng, fn
        self.reads, self.writes = tuple(reads), tuple(writes)
        self.dma = dma
        self.deps = set()
        self.inc = None


class KB:

    def __init__(self, nc):
        self.nc = nc
        self.ops = []
        self.buf = {}

    def _bs(self, key):
        st = self.buf.get(key)
        if st is None:
            st = {"w": None, "r": {}, "p": set()}
            self.buf[key] = st
        return st

    def fence(self, src_keys, dst_keys):
        deps = set()
        for sk in src_keys:
            st = self._bs(sk)
            if st["w"] is not None:
                deps.add(st["w"])
            deps.update(st["r"].values())
        for dk in dst_keys:
            self._bs(dk)["p"].update(deps)

    def op(self, eng, fn, reads=(), writes=(), dma=False):
        idx = len(self.ops)
        if dma:
            dma = "ch_" + (writes[0] if writes else reads[0])
        o = _Op(eng, fn, reads, writes, dma)
        for b in o.reads:
            st = self._bs(b)
            if st["p"]:
                o.deps.update(st["p"])
                st["p"] = set()
            if st["w"] is not None:
                o.deps.add(st["w"])
        for b in o.writes:
            st = self._bs(b)
            if st["p"]:
                o.deps.update(st["p"])
                st["p"] = set()
            if st["w"] is not None:
                o.deps.add(st["w"])
            o.deps.update(st["r"].values())
        o.deps.discard(idx)
        self.ops.append(o)
        rk = eng + ("D" if dma else "")
        for b in o.reads:
            self._bs(b)["r"][rk] = idx
        for b in o.writes:
            st = self._bs(b)
            st["w"] = idx
            st["r"] = {}
        return idx

    def dma(self, eng, out, in_, reads=(), writes=()):
        return self.op(eng, lambda e, out=out, in_=in_: e.dma_start(out=out, in_=in_),
                       reads=reads, writes=writes, dma=True)

    def build(self):
        needed = set()
        for idx, o in enumerate(self.ops):
            needed.update(o.deps)
            if o.dma:
                needed.add(idx)
        counters = {}
        for idx in sorted(needed):
            o = self.ops[idx]
            s = o.dma if o.dma else ("P_" + o.eng)
            step = 16 if o.dma else 1
            counters[s] = counters.get(s, 0) + step
            o.inc = (s, counters[s])
        stack = contextlib.ExitStack()
        sems = {s: stack.enter_context(self.nc.semaphore(s)) for s in counters}
        hw = {}
        streams = {e: [] for e in ENGINES}
        for o in self.ops:
            waits = {}
            for d in o.deps:
                s, v = self.ops[d].inc
                if hw.get((o.eng, s), 0) >= v:
                    continue
                if waits.get(s, 0) < v:
                    waits[s] = v
            for s, v in sorted(waits.items()):
                streams[o.eng].append(lambda e, h=sems[s], v=v: e.wait_ge(h, v))
                hw[(o.eng, s)] = max(hw.get((o.eng, s), 0), v)
            if o.inc is not None:
                s, _ = o.inc
                step = 16 if o.dma else 1
                streams[o.eng].append(
                    lambda e, fn=o.fn, h=sems[s], n=step: fn(e).then_inc(h, n))
            else:
                streams[o.eng].append(o.fn)
        last_dma = None
        for o in reversed(self.ops):
            if o.dma and o.inc is not None:
                last_dma = o.inc
                break
        with self.nc.Block() as block:
            for eng_name in ENGINES:
                st = streams[eng_name]
                if not st and eng_name != "sync":
                    continue

                def make(st, eng_name=eng_name):
                    def _(eng):
                        for fn in st:
                            fn(eng)
                        if eng_name == "sync" and last_dma is not None:
                            eng.wait_ge(sems[last_dma[0]], last_dma[1])
                    return _
                getattr(block, eng_name)(make(st))
        return stack


def ts(i, n):
    return slice(i * n, (i + 1) * n)


def l0_chunk_off(h):
    if h < 4:
        return h, 0
    if h < 8:
        return h - 4, 64
    if h < 12:
        return h - 4, 0
    return h - 8, 64


def l0_wqb_perm():
    perm = np.zeros(D, dtype=np.int64)
    for h in range(NH):
        c, o = l0_chunk_off(h)
        perm[c * 128 + o: c * 128 + o + 64] = np.arange(h * 64, h * 64 + 64)
    return perm


def build(T=2048, n_cores=8):
    nc = bass.Bass("TRN2", target_bir_lowering=False)
    k = KB(nc)
    NT = T // 128
    NTG = T // 512
    assert T % 512 == 0

    def dram(name, shape, kind=None):
        if kind:
            return nc.dram_tensor(name, shape, F32, kind=kind).ap()
        return nc.dram_tensor(name, shape, F32).ap()

    x_d = dram("x", [T, D], "ExternalInput")
    wqa_d = dram("wqa", [D, 64], "ExternalInput")
    wqb_d = dram("wqb", [64, D], "ExternalInput")
    wkva_d = dram("wkva", [D, 32], "ExternalInput")
    wkvb_d = dram("wkvb", [32, 512], "ExternalInput")
    wout0_d = dram("wout0", [D, D], "ExternalInput")
    bout0_d = dram("bout0", [128, 8], "ExternalInput")
    winT0_d = dram("winT0", [DFF0, D], "ExternalInput")
    bin0_d = dram("bin0", [128, 32], "ExternalInput")
    wo0_d = dram("wo0", [DFF0, D], "ExternalInput")
    bo0bc_d = dram("bo0bc", [128, D], "ExternalInput")
    wqT_d = dram("wqT", [D, D], "ExternalInput")
    wkT_d = dram("wkT", [D, D], "ExternalInput")
    wvT_d = dram("wvT", [D, D], "ExternalInput")
    woT_d = dram("woT", [D, D], "ExternalInput")
    w1T_d = dram("w1T", [D, HID1], "ExternalInput")
    b1bc_d = dram("b1bc", [128, HID1], "ExternalInput")
    w2T_d = dram("w2T", [HID1, D], "ExternalInput")
    ident_d = dram("ident", [128, 128], "ExternalInput")
    out_d = dram("out", [T, D], "ExternalOutput")

    x1sp = dram("x1sp", [T, D])
    xn1sp = dram("xn1sp", [T, D])
    ssp = dram("ssp", [T, D])
    hidsp = dram("hidsp", [DFF0, T])
    wintsp = dram("wintsp", [D, DFF0])
    wpsp = dram("wpsp", [DFF0, D])
    qlatsp = dram("qlatsp", [96, T])
    cc_in_a = dram("cc_in_a", [1, 1])
    cc_out_a = dram("cc_out_a", [1, 1])
    cc_in_b = dram("cc_in_b", [1, 1])
    cc_out_b = dram("cc_out_b", [1, 1])

    def sb(name, shape, dt=F32):
        return nc.alloc_sbuf_tensor(name, shape, dt).ap()

    AW = max(8 * T, 16384)
    A1 = sb("A1", [128, AW])
    A2 = sb("A2", [128, AW])
    ident = sb("ident_s", [128, 128])
    maskd = sb("maskd", [128, 128])
    tiles = sb("tiles", [128, 4096])
    tile_a, tile_b = tiles[:, 0:1024], tiles[:, 1024:2048]
    tile_c, tile_d = tiles[:, 2048:3072], tiles[:, 3072:4096]
    TILEK = ["tile_a", "tile_b", "tile_c", "tile_d"]
    TILEV = [tile_a, tile_b, tile_c, tile_d]
    stg = [sb(f"stg{i}", [128, 1024]) for i in range(2)]
    wsm = [sb(f"wsm{i}", [128, 1024]) for i in range(2)]
    pstrip = sb("pstrip", [128, max(T, 2048)])
    ptstrip = sb("ptstrip", [128, max(T, 2048)])
    qTt = sb("qTt", [128, 1024])
    wqa_s = sb("wqa_s", [128, 8 * 64])
    wkva_s = sb("wkva_s", [128, 8 * 32])
    bout0_s = sb("bout0_s", [128, 8])
    bin0_s = sb("bin0_s", [128, 32])
    scal = sb("scal", [128, 16])
    stats = sb("stats", [128, 16])
    lnm = sb("lnm", [128, 2 * NT])
    srow = sb("srow", [1, 64])
    srow2 = sb("srow2", [1, 128])
    ones1 = sb("ones1", [1, 128])
    invf = sb("invf", [128, 32])
    tcol = sb("tcol", [128, 1])
    scmul = sb("scmul", [128, 32])
    wstat = sb("wstat", [128, 8])
    rsc = sb("rsc", [128, 64])
    recips = sb("recips", [128, 16])
    mask1h = nc.alloc_sbuf_tensor("mask1h", [128, P_TREE * G_TREE], BF16).ap()
    treb = nc.alloc_sbuf_tensor("treb", [128, 4 * P_TREE * DEC_G], BF16).ap()
    tre = [treb[:, ts(i, P_TREE * DEC_G)] for i in range(4)]

    ktl0 = A2[:, 0:2 * T]
    vl0 = A2[:, 2 * T:2 * T + NT * 256]
    cosq = A2[:, 6 * T:6 * T + NT * 32]
    sinq = A2[:, 6 * T + NT * 32:6 * T + 2 * NT * 32]
    bo0bc_s = A2[:, 0:D]
    qlat2 = A2[0:64, 8192:8192 + T]
    kvlat2 = A2[64:96, 8192:8192 + T]
    wqb_s = A2[0:64, 8192 + T:8192 + T + D]
    wkvb_s = A2[64:96, 8192 + T + D:8192 + T + D + 512]
    actdec = A1[:, 0:P_TREE * DEC_W]
    trm = A1[:, 5456:5456 + P_TREE * DEC_G]
    trc = A1[:, 6820:6820 + P_TREE * DEC_G]
    trt = A1[:, 8184:8184 + P_TREE * DEC_G]
    bslc = [A1[:, 9600 + 512 * i: 9600 + 512 * (i + 1)] for i in range(2)]
    aslc = ([A1[:, 10624 + 512 * i: 10624 + 512 * (i + 1)] for i in range(2)]
            + [A1[:, 13696 + 512 * i: 13696 + 512 * (i + 1)] for i in range(2)])
    mstg = [A1[:, 11648 + 512 * i: 11648 + 512 * (i + 1)] for i in range(2)]
    xn1t = A1[:, 12672:12672 + 1024]
    wbufA = [A2[:, ts(i, 4096)] for i in range(2)]
    wbufB = [A2[:, 8192 + 4096 * i: 8192 + 4096 * (i + 1)] for i in range(2)]
    P9C_A1 = ["actdec", "trm", "trc", "trt", "bslc0", "bslc1", "aslc0", "aslc1",
              "aslc2", "aslc3", "mstg0", "mstg1", "xn1t"]
    P9C_A2 = ["wbufA0", "wbufA1", "wbufB0", "wbufB1"]

    psA = nc.alloc_psum_tensor("psA", [128, 2048], F32).ap()
    psB = nc.alloc_psum_tensor("psB", [128, 1024], F32).ap()
    psC = nc.alloc_psum_tensor("psC", [128, 512], F32).ap()
    psD = nc.alloc_psum_tensor("psD", [128, 512], F32).ap()
    PSA = ["psA0", "psA1", "psA2", "psA3"]
    PSB = ["psB0", "psB1"]

    V, S, TE, SY, GP = "vector", "scalar", "tensor", "sync", "gpsimd"

    def t_copy(eng, out, in_, reads, writes):
        k.op(eng, lambda e: e.tensor_copy(out, in_), reads=reads, writes=writes)

    def emit_round(buf, scr, key, skey):
        k.op(V, lambda e: e.tensor_scalar(scr, buf, C32, C32, ALU.add, ALU.subtract),
             reads=[key], writes=[skey])
        k.op(V, lambda e: e.tensor_scalar(buf, buf, C64, C64, ALU.add, ALU.subtract),
             reads=[key], writes=[key])
        k.op(V, lambda e: e.tensor_scalar(buf, buf, scal[:, 1:2], None, ALU.mult),
             reads=[key, "w64"], writes=[key])
        k.op(V, lambda e: e.scalar_tensor_tensor(buf, scr, scal[:, 0:1], buf,
                                                 ALU.mult, ALU.add),
             reads=[skey, key, "w32"], writes=[key])

    def bc_scalar(src_ap, src_key, dst_col):
        k.op(TE, lambda e: e.matmul(psD[:, 0:1], ones1, src_ap, start=True, stop=True),
             reads=["ones1", src_key], writes=["psD"])
        t_copy(V, scal[:, dst_col:dst_col + 1], psD[:, 0:1], ["psD"],
               [f"scal{dst_col}"])

    def part_max(col_ap, col_key, out_ap, out_key):
        k.op(TE, lambda e: e.transpose(psD[0:1, 0:128], col_ap, ident),
             reads=[col_key, "ident"], writes=["psD"])
        t_copy(V, srow2, psD[0:1, 0:128], ["psD"], ["srow2"])
        k.op(V, lambda e: e.reduce_max(out_ap, srow2, axis=AX.X),
             reads=["srow2"], writes=[out_key])

    def xnT_ap(buf, c, j0, n):
        assert j0 % 128 == 0 and n % 128 == 0
        return bass.AP(buf.tensor, buf.offset + (j0 // 128) * 1024 + 128 * c,
                       [[buf.ap[0][0], 128], [1024, n // 128], [1, 128]])

    def mm_group(out_ap, pairs, psum_key, read_keys):
        def fn(e, pairs=pairs, out_ap=out_ap):
            ins = None
            n = len(pairs)
            for i, (lh, rh) in enumerate(pairs):
                ins = e.matmul(out_ap, lh, rh, start=(i == 0), stop=(i == n - 1))
            return ins
        k.op(TE, fn, reads=list(read_keys), writes=[psum_key])

    def emit_transpose_tile(src, src_key, dst, dst_key):
        for half in range(2):
            for c4 in range(4):
                c = 4 * half + c4
                k.op(TE, lambda e, c=c, c4=c4, src=src: e.transpose(
                    psC[:, ts(c4, 128)], src[:, ts(c, 128)], ident),
                     reads=[src_key, "ident"], writes=["psC"])
            k.op(V, lambda e, half=half, dst=dst: e.tensor_copy(dst[:, ts(half, 512)],
                                                                psC),
                 reads=["psC"], writes=[dst_key])

    k.dma(SY, ident, ident_d, writes=["ident"])
    k.dma(SY, bout0_s, bout0_d, writes=["bout0"])
    k.dma(SY, bin0_s, bin0_d, writes=["bin0"])
    k.dma(SY, wqa_s, bass.AP(wqa_d.tensor, 0, [[64, 128], [64 * 128, 8], [1, 64]]),
          writes=["wqa"])
    k.dma(SY, wkva_s, bass.AP(wkva_d.tensor, 0, [[32, 128], [32 * 128, 8], [1, 32]]),
          writes=["wkva"])

    k.op(GP, lambda e: e.memset(ones1, 1.0), writes=["ones1"])
    k.op(GP, lambda e: e.memset(scal[0:1, 6:7], 1.0), writes=["sc6"])
    k.op(V, lambda e: e.tensor_scalar(scal[0:1, 7:8], scal[0:1, 6:7],
                                      9.313225746154785e-10, 1.0, ALU.add,
                                      ALU.subtract),
         reads=["sc6"], writes=["sc7"])
    k.op(V, lambda e: e.tensor_scalar(scal[0:1, 7:8], scal[0:1, 7:8],
                                      1073741824.0, None, ALU.mult),
         reads=["sc7"], writes=["sc7"])
    k.op(V, lambda e: e.tensor_scalar(scal[0:1, 8:9], scal[0:1, 7:8],
                                      -1.0, 1.0, ALU.mult, ALU.add),
         reads=["sc7"], writes=["sc8"])
    k.op(TE, lambda e: e.matmul(psD[:, 0:2], ones1, scal[0:1, 7:9], start=True,
                                stop=True),
         reads=["ones1", "sc7", "sc8"], writes=["psD"])
    t_copy(V, scal[:, 1:2], psD[:, 0:1], ["psD"], ["w64"])
    t_copy(V, scal[:, 0:1], psD[:, 1:2], ["psD"], ["w32"])

    k.op(GP, lambda e: e.iota(maskd, [[1, 128]], channel_multiplier=-1,
                              allow_small_or_imprecise_dtypes=True),
         writes=["maskd"])
    k.op(V, lambda e: e.tensor_scalar(maskd, maskd, 0.5, NEGBIG, ALU.is_gt, ALU.mult),
         reads=["maskd"], writes=["maskd"])

    for t in range(NT):
        k.dma(SY, tile_a, x_d[ts(t, 128), :], writes=["tile_a"])
        k.op(V, lambda e: e.reduce_sum(stats[:, 0:1], tile_a, axis=AX.X),
             reads=["tile_a"], writes=["st0"])
        k.op(S, lambda e: e.activation(tile_c, tile_a, AF.Square,
                                       accum_out=stats[:, 1:2]),
             reads=["tile_a"], writes=["tile_c", "st1"])
        k.op(V, lambda e: e.tensor_scalar(stats[:, 2:3], stats[:, 0:1], 1.0 / D, None,
                                          ALU.mult),
             reads=["st0"], writes=["st2"])
        k.op(V, lambda e: e.tensor_scalar(stats[:, 4:5], stats[:, 2:3], stats[:, 2:3],
                                          None, ALU.mult),
             reads=["st2"], writes=["st4"])
        k.op(V, lambda e: e.tensor_scalar(stats[:, 5:6], stats[:, 1:2], 1.0 / D, 1e-5,
                                          ALU.mult, ALU.add),
             reads=["st1"], writes=["st5"])
        k.op(V, lambda e: e.tensor_sub(stats[:, 5:6], stats[:, 5:6], stats[:, 4:5]),
             reads=["st5", "st4"], writes=["st5"])
        k.op(V, lambda e: e.reciprocal(stats[:, 6:7], stats[:, 5:6]),
             reads=["st5"], writes=["st6"])
        k.op(S, lambda e: e.activation(stats[:, 7:8], stats[:, 6:7], AF.Sqrt),
             reads=["st6"], writes=["st7"])
        k.op(V, lambda e: e.tensor_scalar(tile_b, tile_a, stats[:, 2:3], stats[:, 7:8],
                                          ALU.subtract, ALU.mult),
             reads=["tile_a", "st2", "st7"], writes=["tile_b"])
        k.op(V, lambda e: e.reduce_max(stats[:, 8:9], tile_b, axis=AX.X,
                                       apply_absolute_value=True),
             reads=["tile_b"], writes=["st8"])
        if t == 0:
            t_copy(V, stats[:, 9:10], stats[:, 8:9], ["st8"], ["st9"])
        else:
            k.op(V, lambda e: e.tensor_max(stats[:, 9:10], stats[:, 9:10],
                                           stats[:, 8:9]),
                 reads=["st9", "st8"], writes=["st9"])
        for half in range(2):
            for c4 in range(4):
                c = 4 * half + c4
                k.op(TE, lambda e, c=c, c4=c4: e.transpose(
                    psC[:, ts(c4, 128)], tile_b[:, ts(c, 128)], ident),
                     reads=["tile_b", "ident"], writes=["psC"])
            k.op(V, lambda e, t=t, half=half: e.tensor_copy(
                A1[:, 1024 * t + 512 * half: 1024 * t + 512 * half + 512], psC),
                 reads=["psC"], writes=["A1"])

    part_max(stats[:, 9:10], "st9", stats[0:1, 10:11], "st10")
    k.op(V, lambda e: e.tensor_scalar(stats[0:1, 10:11], stats[0:1, 10:11], 1e-8,
                                      None, ALU.max),
         reads=["st10"], writes=["st10"])
    k.dma(GP, cc_in_a, stats[0:1, 10:11], reads=["st10"], writes=["cca"])
    k.op(GP, lambda e: e.collective_compute(
        "AllReduce", ALU.max, replica_groups=[list(range(n_cores))],
        ins=[cc_in_a.opt()], outs=[cc_out_a.opt()]),
         reads=["cca"], writes=["ccao"])
    k.dma(GP, stats[0:1, 11:12], cc_out_a, reads=["ccao"], writes=["st11"])
    k.op(V, lambda e: e.reciprocal(stats[0:1, 12:13], stats[0:1, 11:12]),
         reads=["st11"], writes=["st12"])
    k.op(V, lambda e: e.tensor_scalar(stats[0:1, 12:13], stats[0:1, 12:13], 32767.0,
                                      None, ALU.mult),
         reads=["st12"], writes=["st12"])
    k.op(V, lambda e: e.reciprocal(stats[0:1, 13:14], stats[0:1, 12:13]),
         reads=["st12"], writes=["st13"])
    bc_scalar(stats[0:1, 12:13], "st12", 2)
    bc_scalar(stats[0:1, 13:14], "st13", 3)

    for sl in range(NTG):
        pairs = [(wqa_s[:, ts(c, 64)], xnT_ap(A1, c, sl * 512, 512)) for c in range(8)]
        mm_group(psD[0:64, :], pairs, "psD", ["wqa", "A1"])
        t_copy(V, stg[0][0:64, 0:512], psD[0:64, :], ["psD"], ["stg0"])
        k.dma(SY, qlatsp[0:64, ts(sl, 512)], stg[0][0:64, 0:512],
              reads=["stg0"], writes=["qlatsp"])
        pairs = [(wkva_s[:, ts(c, 32)], xnT_ap(A1, c, sl * 512, 512)) for c in range(8)]
        mm_group(psC[0:32, :], pairs, "psC", ["wkva", "A1"])
        t_copy(V, stg[1][0:32, 0:512], psC[0:32, :], ["psC"], ["stg1"])
        k.dma(SY, qlatsp[64:96, ts(sl, 512)], stg[1][0:32, 0:512],
              reads=["stg1"], writes=["qlatsp"])

    for j in range(8 * T // 1024):
        sl = A2[:, ts(j, 1024)]
        src = A1[:, ts(j, 1024)]
        k.op(V, lambda e, sl=sl, src=src: e.tensor_scalar(sl, src, scal[:, 2:3], None,
                                                          ALU.mult),
             reads=["A1", "scal2"], writes=["A2"])
        emit_round(sl, tile_c, "A2", "tile_c")
        k.op(V, lambda e, sl=sl: e.tensor_scalar(sl, sl, 32767.0, -32768.0,
                                                 ALU.min, ALU.max),
             reads=["A2"], writes=["A2"])

    for c in range(32):
        w = wsm[c % 2]
        wk = f"wsm{c % 2}"
        k.dma(SY, w, winT0_d[ts(c, 128), :], writes=[wk])
        k.op(V, lambda e, w=w: e.reduce_max(wstat[:, 0:1], w, axis=AX.X,
                                            apply_absolute_value=True),
             reads=[wk], writes=["ws0"])
        t_copy(V, scmul[:, c:c + 1], wstat[:, 0:1], ["ws0"], ["scmul"])
        k.op(V, lambda e: e.reciprocal(wstat[:, 1:2], wstat[:, 0:1]),
             reads=["ws0"], writes=["ws1"])
        k.op(V, lambda e: e.tensor_scalar(wstat[:, 1:2], wstat[:, 1:2], 127.0, None,
                                          ALU.mult),
             reads=["ws1"], writes=["ws1"])
        k.op(V, lambda e, w=w: e.tensor_scalar(w, w, wstat[:, 1:2], None, ALU.mult),
             reads=[wk, "ws1"], writes=[wk])
        emit_round(w, stg[0], wk, "stg0")
        k.op(V, lambda e, w=w: e.tensor_scalar(w, w, 127.0, -128.0, ALU.min, ALU.max),
             reads=[wk], writes=[wk])
        for half in range(2):
            for c4 in range(4):
                cc = 4 * half + c4
                k.op(TE, lambda e, w=w, cc=cc, c4=c4: e.transpose(
                    psC[:, ts(c4, 128)], w[:, ts(cc, 128)], ident),
                     reads=[wk, "ident"], writes=["psC"])
            k.op(V, lambda e, half=half: e.tensor_copy(stg[1][:, ts(half, 512)], psC),
                 reads=["psC"], writes=["stg1"])
        k.dma(SY, bass.AP(wintsp.tensor, 128 * c,
                          [[DFF0, 128], [128 * DFF0, 8], [1, 128]]),
              stg[1].rearrange("p (a b) -> p a b", b=128),
              reads=["stg1"], writes=["wintsp"])
    k.op(V, lambda e: e.tensor_scalar(scmul, scmul, scal[:, 3:4], 1.0 / 127.0,
                                      ALU.mult, ALU.mult),
         reads=["scmul", "scal3"], writes=["scmul"])

    for grp in range(NTG):
        for m in range(32):
            w = wsm[m % 2]
            wk = f"wsm{m % 2}"
            k.dma(SY, w, bass.AP(wintsp.tensor, 128 * m,
                                 [[DFF0, 128], [128 * DFF0, 8], [1, 128]]),
                  reads=["wintsp"], writes=[wk])
            ps, pk = (psC, "psC") if m % 2 == 0 else (psD, "psD")
            pairs = [(w[:, ts(c, 128)], xnT_ap(A2, c, grp * 512, 512))
                     for c in range(8)]
            mm_group(ps, pairs, pk, [wk, "A2"])
            st_ = stg[m % 2]
            sk = f"stg{m % 2}"
            k.op(S, lambda e, ps=ps, st_=st_, m=m: e.activation(
                st_[:, 0:512], ps, AF.Gelu, bias=bin0_s[:, m:m + 1],
                scale=scmul[:, m:m + 1]),
                 reads=[pk, "bin0", "scmul"], writes=[sk])
            k.op(V, lambda e, st_=st_: e.reduce_max(wstat[:, 2:3], st_[:, 0:512],
                                                    axis=AX.X,
                                                    apply_absolute_value=True),
                 reads=[sk], writes=["ws2"])
            if grp == 0 and m == 0:
                t_copy(V, wstat[:, 3:4], wstat[:, 2:3], ["ws2"], ["ws3"])
            else:
                k.op(V, lambda e: e.tensor_max(wstat[:, 3:4], wstat[:, 3:4],
                                               wstat[:, 2:3]),
                     reads=["ws3", "ws2"], writes=["ws3"])
            k.dma(SY, bass.AP(hidsp.tensor, 128 * m * T + 512 * grp,
                              [[T, 128], [1, 512]]),
                  st_[:, 0:512], reads=[sk], writes=["hidsp"])

    part_max(wstat[:, 3:4], "ws3", wstat[0:1, 4:5], "ws4")
    k.op(V, lambda e: e.tensor_scalar(wstat[0:1, 4:5], wstat[0:1, 4:5], 1e-8, None,
                                      ALU.max),
         reads=["ws4"], writes=["ws4"])
    k.dma(GP, cc_in_b, wstat[0:1, 4:5], reads=["ws4"], writes=["ccb"])
    k.op(GP, lambda e: e.collective_compute(
        "AllReduce", ALU.max, replica_groups=[list(range(n_cores))],
        ins=[cc_in_b.opt()], outs=[cc_out_b.opt()]),
         reads=["ccb"], writes=["ccbo"])
    k.dma(GP, wstat[0:1, 5:6], cc_out_b, reads=["ccbo"], writes=["ws5"])
    k.op(V, lambda e: e.reciprocal(wstat[0:1, 6:7], wstat[0:1, 5:6]),
         reads=["ws5"], writes=["ws6"])
    k.op(V, lambda e: e.tensor_scalar(wstat[0:1, 6:7], wstat[0:1, 6:7], 32767.0, None,
                                      ALU.mult),
         reads=["ws6"], writes=["ws6"])
    k.op(V, lambda e: e.reciprocal(wstat[0:1, 7:8], wstat[0:1, 6:7]),
         reads=["ws6"], writes=["ws7"])
    bc_scalar(wstat[0:1, 6:7], "ws6", 4)
    bc_scalar(wstat[0:1, 7:8], "ws7", 5)

    for c in range(32):
        w = wsm[c % 2]
        wk = f"wsm{c % 2}"
        k.dma(SY, w, wo0_d[ts(c, 128), :], writes=[wk])
        k.op(V, lambda e, w=w: e.reduce_max(wstat[:, 0:1], w, axis=AX.X,
                                            apply_absolute_value=True),
             reads=[wk], writes=["ws0"])
        k.op(V, lambda e: e.reciprocal(wstat[:, 1:2], wstat[:, 0:1]),
             reads=["ws0"], writes=["ws1"])
        k.op(V, lambda e: e.tensor_scalar(wstat[:, 1:2], wstat[:, 1:2], 127.0, None,
                                          ALU.mult),
             reads=["ws1"], writes=["ws1"])
        k.op(V, lambda e, w=w: e.tensor_scalar(w, w, wstat[:, 1:2], None, ALU.mult),
             reads=[wk, "ws1"], writes=[wk])
        emit_round(w, stg[0], wk, "stg0")
        k.op(V, lambda e, w=w: e.tensor_scalar(w, w, 127.0, -128.0, ALU.min, ALU.max),
             reads=[wk], writes=[wk])
        k.op(V, lambda e: e.tensor_scalar(wstat[:, 0:1], wstat[:, 0:1], scal[:, 5:6],
                                          1.0 / 127.0, ALU.mult, ALU.mult),
             reads=["ws0", "scal5"], writes=["ws0"])
        k.op(V, lambda e, w=w: e.tensor_scalar(w, w, wstat[:, 0:1], None, ALU.mult),
             reads=[wk, "ws0"], writes=[wk])
        k.dma(SY, wpsp[ts(c, 128), :], w, reads=[wk], writes=["wpsp"])

    gbank = ([psA[:, ts(i, 512)] for i in range(4)]
             + [psB[:, ts(i, 512)] for i in range(2)] + [psC, psD])
    gbkey = PSA + PSB + ["psC", "psD"]
    k.fence(["A2"], ["bo0bc"])
    k.dma(SY, bo0bc_s, bo0bc_d, writes=["bo0bc"])
    for grp in range(NTG):
        for c in range(32):
            h = stg[c % 2]
            hk = f"stg{c % 2}"
            k.dma(SY, h[:, 0:512],
                  bass.AP(hidsp.tensor, 128 * c * T + 512 * grp, [[T, 128], [1, 512]]),
                  reads=["hidsp"], writes=[hk])
            k.op(V, lambda e, h=h: e.tensor_scalar(h[:, 0:512], h[:, 0:512],
                                                   scal[:, 4:5], None, ALU.mult),
                 reads=[hk, "scal4"], writes=[hk])
            emit_round(h[:, 0:512], tile_c[:, 0:512], hk, "tile_c")
            k.op(V, lambda e, h=h: e.tensor_scalar(h[:, 0:512], h[:, 0:512],
                                                   32767.0, -32768.0, ALU.min,
                                                   ALU.max),
                 reads=[hk], writes=[hk])
            wp = wsm[c % 2]
            wpk = f"wsm{c % 2}"
            k.dma(SY, wp, wpsp[ts(c, 128), :], reads=["wpsp"], writes=[wpk])
            for sub in range(4):
                for half in range(2):
                    bi = sub * 2 + half
                    k.op(TE, lambda e, h=h, wp=wp, sub=sub, half=half, c=c, bi=bi:
                         e.matmul(gbank[bi], h[:, ts(sub, 128)], wp[:, ts(half, 512)],
                                  start=(c == 0), stop=(c == 31)),
                         reads=[hk, wpk], writes=[gbkey[bi]])
        for sub in range(4):
            tt = grp * 4 + sub
            for half in range(2):
                bi = sub * 2 + half
                k.op(V, lambda e, bi=bi, half=half: e.tensor_add(
                    tile_d[:, ts(half, 512)], gbank[bi], bo0bc_s[:, ts(half, 512)]),
                     reads=[gbkey[bi], "bo0bc"], writes=["tile_d"])
            k.op(S, lambda e, tt=tt: e.activation(A1[:, ts(tt, 1024)], tile_d,
                                                  AF.Silu),
                 reads=["tile_d"], writes=["A1"])

    k.fence(["A2", "bo0bc"], ["ktl0", "vl0", "cosq", "sinq", "qlat", "kvlat",
                              "wqb", "wkvb"])
    k.dma(SY, qlat2, qlatsp[0:64, :], reads=["qlatsp"], writes=["qlat"])
    k.dma(SY, kvlat2, qlatsp[64:96, :], reads=["qlatsp"], writes=["kvlat"])
    k.dma(SY, wqb_s, wqb_d, writes=["wqb"])
    k.dma(SY, wkvb_s, wkvb_d, writes=["wkvb"])
    k.op(GP, lambda e: e.iota(srow[0:1, 0:32], [[1, 32]], channel_multiplier=0,
                              allow_small_or_imprecise_dtypes=True),
         writes=["srow"])
    k.op(GP, lambda e: e.memset(srow[0:1, 40:41], 10000.0), writes=["sr40"])
    k.op(S, lambda e: e.activation(srow[0:1, 41:42], srow[0:1, 40:41], AF.Ln),
         reads=["sr40"], writes=["sr41"])
    k.op(V, lambda e: e.tensor_scalar(srow[0:1, 0:32], srow[0:1, 0:32],
                                      srow[0:1, 41:42], -1.0 / 32.0, ALU.mult,
                                      ALU.mult),
         reads=["srow", "sr41"], writes=["srow"])
    k.op(S, lambda e: e.activation(srow[0:1, 0:32], srow[0:1, 0:32], AF.Exp),
         reads=["srow"], writes=["srow"])
    k.op(TE, lambda e: e.matmul(psD[:, 0:32], ones1, srow[0:1, 0:32], start=True,
                                stop=True),
         reads=["ones1", "srow"], writes=["psD"])
    t_copy(V, invf, psD[:, 0:32], ["psD"], ["invf"])
    for t in range(NT):
        ang = rsc[:, 0:32]
        red = rsc[:, 32:64]
        k.op(GP, lambda e, t=t: e.iota(tcol, [[0, 1]], base=128 * t,
                                       channel_multiplier=1,
                                       allow_small_or_imprecise_dtypes=True),
             writes=["tcol"])
        k.op(V, lambda e: e.tensor_scalar(ang, invf, tcol[:, 0:1], None, ALU.mult),
             reads=["invf", "tcol"], writes=["rsc0"])
        k.op(V, lambda e: e.tensor_scalar(tile_c[:, 0:32], ang, 0.15915494309189535,
                                          None, ALU.mult),
             reads=["rsc0"], writes=["tile_c"])
        emit_round(tile_c[:, 0:32], tile_d[:, 0:32], "tile_c", "tile_d")
        k.op(V, lambda e: e.scalar_tensor_tensor(red, tile_c[:, 0:32], -TPH, ang,
                                                 ALU.mult, ALU.add),
             reads=["tile_c", "rsc0"], writes=["rsc1"])
        k.op(V, lambda e: e.scalar_tensor_tensor(red, tile_c[:, 0:32], -TPL, red,
                                                 ALU.mult, ALU.add),
             reads=["tile_c", "rsc1"], writes=["rsc1"])
        k.op(V, lambda e: e.tensor_scalar(red, red, PICLIP, -PICLIP, ALU.min,
                                          ALU.max),
             reads=["rsc1"], writes=["rsc1"])
        k.op(V, lambda e: e.tensor_scalar(ang, red, PHH, PHL, ALU.add, ALU.add),
             reads=["rsc1"], writes=["rsc0"])
        k.op(V, lambda e: e.tensor_scalar(tile_c[:, 0:32], ang, 3.1415927410125732,
                                          None, ALU.is_gt),
             reads=["rsc0"], writes=["tile_c"])
        k.op(V, lambda e: e.scalar_tensor_tensor(ang, tile_c[:, 0:32], -TPH, ang,
                                                 ALU.mult, ALU.add),
             reads=["tile_c", "rsc0"], writes=["rsc0"])
        k.op(V, lambda e: e.scalar_tensor_tensor(ang, tile_c[:, 0:32], -TPL, ang,
                                                 ALU.mult, ALU.add),
             reads=["tile_c", "rsc0"], writes=["rsc0"])
        k.op(V, lambda e: e.tensor_scalar(ang, ang, PICLIP, -PICLIP, ALU.min,
                                          ALU.max),
             reads=["rsc0"], writes=["rsc0"])
        k.op(S, lambda e, t=t: e.activation(cosq[:, ts(t, 32)], ang, AF.Sin),
             reads=["rsc0"], writes=["cosq"])
        k.op(S, lambda e, t=t: e.activation(sinq[:, ts(t, 32)], red, AF.Sin),
             reads=["rsc1"], writes=["sinq"])

    def emit_rope(src, src_key, dst, dst_key, t, nh):
        cB = bass.AP(cosq.tensor, cosq.offset + 32 * t,
                     [[cosq.ap[0][0], 128], [0, nh], [1, 32]])
        sB = bass.AP(sinq.tensor, sinq.offset + 32 * t,
                     [[sinq.ap[0][0], 128], [0, nh], [1, 32]])

        def sl(buf, off):
            return bass.AP(buf.tensor, buf.offset + off,
                           [[buf.ap[0][0], 128], [64, nh], [2, 32]])
        se, so = sl(src, 0), sl(src, 1)
        de, do = sl(dst, 0), sl(dst, 1)
        scr = bass.AP(tile_d.tensor, tile_d.offset,
                      [[tile_d.ap[0][0], 128], [32, nh], [1, 32]])
        k.op(V, lambda e: e.tensor_mul(de, se, cB), reads=[src_key, "cosq"],
             writes=[dst_key])
        k.op(V, lambda e: e.tensor_mul(scr, so, sB), reads=[src_key, "sinq"],
             writes=["tile_d"])
        k.op(V, lambda e: e.tensor_sub(de, de, scr), reads=[dst_key, "tile_d"],
             writes=[dst_key])
        k.op(V, lambda e: e.tensor_mul(do, se, sB), reads=[src_key, "sinq"],
             writes=[dst_key])
        k.op(V, lambda e: e.tensor_mul(scr, so, cB), reads=[src_key, "cosq"],
             writes=["tile_d"])
        k.op(V, lambda e: e.tensor_add(do, do, scr), reads=[dst_key, "tile_d"],
             writes=[dst_key])

    def emit_attn_chunk(t, layer):
        nk = t + 1
        pipe = nk <= 8
        k.fence(["tile_c"], ["tc0", "tc1"])

        def cfg(h):
            if layer == 0:
                kvh = h // 4
                krow = ktl0[64 * (kvh % 2):64 * (kvh % 2) + 64, :]
                kcol0 = (kvh // 2) * T
                kkey = "ktl0"
                vb, vkey = vl0, "vl0"
                vcol = lambda b, kvh=kvh: 256 * b + 64 * kvh
                qc, qo = l0_chunk_off(h)
            else:
                krow = A2[64 * (h % 2):64 * (h % 2) + 64, :]
                kcol0 = (h // 2) * T
                kkey = "A2"
                vb, vkey = A1, "A1"
                vcol = lambda b, h=h: 1024 * b + 64 * h
                qc, qo = h // 2, 64 * (h % 2)
            lh = qTt[qo:qo + 64, ts(qc, 128)]
            par = h % 2 if pipe else 0
            base = 1024 * par
            pskeys = [PSA[2 * par], PSA[2 * par + 1]] if pipe else PSA
            return krow, kcol0, kkey, vb, vkey, vcol, lh, par, base, pskeys

        def emit_qk(h):
            krow, kcol0, kkey, _, _, _, lh, par, base, pskeys = cfg(h)
            nfull = (128 * nk) // 512
            rem = 128 * nk - 512 * nfull

            def qk_span(e, lh, krow, kcol0, k0, k1, remw, base):
                ins = None
                for kt_ in range(k0, k1):
                    ins = e.matmul(
                        psA[:, base + 512 * kt_: base + 512 * (kt_ + 1)], lh,
                        krow[:, kcol0 + 512 * kt_: kcol0 + 512 * (kt_ + 1)],
                        start=True, stop=True)
                if remw:
                    ins = e.matmul(
                        psA[:, base + 512 * k1: base + 512 * k1 + remw], lh,
                        krow[:, kcol0 + 512 * k1: kcol0 + 512 * k1 + remw],
                        start=True, stop=True)
                return ins
            if pipe:
                k.op(TE, lambda e, lh=lh, krow=krow, kcol0=kcol0: qk_span(
                    e, lh, krow, kcol0, 0, nfull, rem, base),
                     reads=["qTt", kkey], writes=pskeys)
            else:
                k.op(TE, lambda e, lh=lh, krow=krow, kcol0=kcol0: qk_span(
                    e, lh, krow, kcol0, 0, 2, 0, 0),
                     reads=["qTt", kkey], writes=["psA0", "psA1"])
                k.op(TE, lambda e, lh=lh, krow=krow, kcol0=kcol0: qk_span(
                    e, lh, krow, kcol0, 2, nfull, rem, 0),
                     reads=["qTt", kkey], writes=["psA2", "psA3"])

        def emit_softmax(h):
            _, _, _, _, _, _, _, par, base, pskeys = cfg(h)
            tc = tile_c[:, 128 * par: 128 * par + 128]
            tck = f"tc{par}"
            scol = 8 * par

            def st(i):
                return stats[:, scol + i: scol + i + 1]

            def stk(i):
                return f"st{i}" if par == 0 else f"sho{i}"
            if pipe:
                pst = pstrip[:, 1024 * par: 1024 * par + 1024]
                psks = [f"pss{par}"]
                k.op(V, lambda e, tc=tc, base=base: e.tensor_add(
                    tc, psA[:, base + 128 * t: base + 128 * (t + 1)], maskd),
                     reads=pskeys + ["maskd"], writes=[tck])
                if t > 0:
                    k.op(V, lambda e, base=base: e.reduce_max(
                        st(0), psA[:, base: base + 128 * t], axis=AX.X),
                         reads=pskeys, writes=[stk(0)])
                    k.op(V, lambda e, tc=tc: e.reduce_max(st(1), tc, axis=AX.X),
                         reads=[tck], writes=[stk(1)])
                    k.op(V, lambda e: e.tensor_max(st(0), st(0), st(1)),
                         reads=[stk(0), stk(1)], writes=[stk(0)])
                else:
                    k.op(V, lambda e, tc=tc: e.reduce_max(st(0), tc, axis=AX.X),
                         reads=[tck], writes=[stk(0)])
                k.op(V, lambda e: e.tensor_scalar(st(2), st(0), -0.125, None,
                                                  ALU.mult),
                     reads=[stk(0)], writes=[stk(2)])
                if t > 0:
                    k.op(S, lambda e, pst=pst, base=base: e.activation(
                        pst[:, 0:128 * t], psA[:, base: base + 128 * t], AF.Exp,
                        bias=st(2), scale=0.125, accum_out=st(3)),
                         reads=pskeys + [stk(2)], writes=psks + [stk(3)])
                k.op(S, lambda e, pst=pst, tc=tc: e.activation(
                    pst[:, 128 * t:128 * (t + 1)], tc, AF.Exp,
                    bias=st(2), scale=0.125, accum_out=st(4)),
                     reads=[tck, stk(2)], writes=psks + [stk(4)])
                if t > 0:
                    k.op(V, lambda e: e.tensor_add(st(5), st(3), st(4)),
                         reads=[stk(3), stk(4)], writes=[stk(5)])
                else:
                    t_copy(V, st(5), st(4), [stk(4)], [stk(5)])
            else:
                hw = 128 * t - 1024
                k.op(V, lambda e, tc=tc: e.tensor_add(
                    tc, psA[:, 128 * t: 128 * (t + 1)], maskd),
                     reads=["psA2", "psA3", "maskd"], writes=[tck])
                k.op(V, lambda e: e.reduce_max(st(0), psA[:, 0:1024], axis=AX.X),
                     reads=["psA0", "psA1"], writes=[stk(0)])
                if hw:
                    k.op(V, lambda e, hw=hw: e.reduce_max(
                        st(1), psA[:, 1024:1024 + hw], axis=AX.X),
                         reads=["psA2", "psA3"], writes=[stk(1)])
                    k.op(V, lambda e: e.tensor_max(st(0), st(0), st(1)),
                         reads=[stk(0), stk(1)], writes=[stk(0)])
                k.op(V, lambda e, tc=tc: e.reduce_max(st(1), tc, axis=AX.X),
                     reads=[tck], writes=[stk(1)])
                k.op(V, lambda e: e.tensor_max(st(0), st(0), st(1)),
                     reads=[stk(0), stk(1)], writes=[stk(0)])
                k.op(V, lambda e: e.tensor_scalar(st(2), st(0), -0.125, None,
                                                  ALU.mult),
                     reads=[stk(0)], writes=[stk(2)])
                k.op(S, lambda e: e.activation(
                    pstrip[:, 0:1024], psA[:, 0:1024], AF.Exp,
                    bias=st(2), scale=0.125, accum_out=st(3)),
                     reads=["psA0", "psA1", stk(2)], writes=["pss0", stk(3)])
                if hw:
                    k.op(S, lambda e, hw=hw: e.activation(
                        pstrip[:, 1024:1024 + hw], psA[:, 1024:1024 + hw],
                        AF.Exp, bias=st(2), scale=0.125, accum_out=st(6)),
                         reads=["psA2", "psA3", stk(2)],
                         writes=["pss1", stk(6)])
                k.op(S, lambda e, tc=tc: e.activation(
                    pstrip[:, 128 * t:128 * (t + 1)], tc, AF.Exp,
                    bias=st(2), scale=0.125, accum_out=st(4)),
                     reads=[tck, stk(2)], writes=["pss1", stk(4)])
                k.op(V, lambda e: e.tensor_add(st(5), st(3), st(4)),
                     reads=[stk(3), stk(4)], writes=[stk(5)])
                if hw:
                    k.op(V, lambda e: e.tensor_add(st(5), st(5), st(6)),
                         reads=[stk(5), stk(6)], writes=[stk(5)])
            k.op(V, lambda e, h=h: e.reciprocal(recips[:, h:h + 1], st(5)),
                 reads=[stk(5)], writes=["recips"])

        def emit_tp_av(h):
            _, _, _, vb, vkey, vcol, _, par, base, pskeys = cfg(h)
            pst = pstrip[:, 1024 * par: 1024 * par + 1024] if pipe else pstrip
            psks = [f"pss{par}"] if pipe else ["pss0", "pss1"]
            ptt = ptstrip[:, 1024 * par: 1024 * par + 1024] if pipe else ptstrip
            ptks = [f"pts{par}"] if pipe else ["pts0", "pts1"]
            nb4 = (nk + 3) // 4
            for b4 in range(nb4):
                nblk = min(4, nk - 4 * b4)

                def tp(e, b4=b4, nblk=nblk, pst=pst):
                    ins = None
                    for i in range(nblk):
                        ins = e.transpose(psC[:, ts(i, 128)],
                                          pst[:, ts(4 * b4 + i, 128)], ident)
                    return ins
                k.op(TE, tp, reads=psks + ["ident"], writes=["psC"])
                k.op(V, lambda e, b4=b4, nblk=nblk, ptt=ptt: e.tensor_copy(
                    ptt[:, 512 * b4: 512 * b4 + 128 * nblk], psC[:, 0:128 * nblk]),
                     reads=["psC"], writes=ptks)

            def av(e, h=h, nk=nk, vcol=vcol, vb=vb, ptt=ptt):
                ins = None
                for b in range(nk):
                    ins = e.matmul(psB[:, 64 * h: 64 * h + 64], ptt[:, ts(b, 128)],
                                   vb[:, vcol(b): vcol(b) + 64],
                                   start=(b == 0), stop=(b == nk - 1))
                return ins
            k.op(TE, av, reads=ptks + [vkey], writes=[PSB[h // 8]])
            if h % 8 == 7:
                for hh in range(h - 7, h + 1):
                    k.op(V, lambda e, hh=hh: e.tensor_scalar(
                        tile_a[:, 64 * hh: 64 * hh + 64],
                        psB[:, 64 * hh: 64 * hh + 64], recips[:, hh:hh + 1], None,
                        ALU.mult),
                         reads=[PSB[h // 8], "recips"], writes=["tile_a"])

        emit_qk(0)
        for h in range(NH):
            emit_softmax(h)
            if h + 1 < NH:
                emit_qk(h + 1)
            emit_tp_av(h)
        k.fence(["tc0", "tc1"], ["tile_c"])

    k.fence(["st8", "st9", "st10", "st11", "st12", "st13"],
            ["sho0", "sho1", "sho2", "sho3", "sho4", "sho5"])
    for t in range(NT):
        k.op(TE, lambda e, t=t: e.matmul(psD, kvlat2[:, ts(t, 128)], wkvb_s,
                                         start=True, stop=True),
             reads=["kvlat", "wkvb"], writes=["psD"])
        t_copy(V, tile_b[:, 0:512], psD, ["psD"], ["tile_b"])
        k.op(V, lambda e, t=t: e.tensor_copy(
            bass.AP(vl0.tensor, vl0.offset + 256 * t,
                    [[vl0.ap[0][0], 128], [64, 4], [1, 64]]),
            bass.AP(tile_b.tensor, tile_b.offset + 1,
                    [[tile_b.ap[0][0], 128], [128, 4], [2, 64]])),
             reads=["tile_b"], writes=["vl0"])
        cB = bass.AP(cosq.tensor, cosq.offset + 32 * t,
                     [[cosq.ap[0][0], 128], [0, 4], [1, 32]])
        sB = bass.AP(sinq.tensor, sinq.offset + 32 * t,
                     [[sinq.ap[0][0], 128], [0, 4], [1, 32]])
        ke = bass.AP(tile_b.tensor, tile_b.offset,
                     [[tile_b.ap[0][0], 128], [128, 4], [4, 32]])
        ko_ = bass.AP(tile_b.tensor, tile_b.offset + 2,
                      [[tile_b.ap[0][0], 128], [128, 4], [4, 32]])
        re_ = bass.AP(tile_c.tensor, tile_c.offset,
                      [[tile_c.ap[0][0], 128], [128, 4], [4, 32]])
        ro_ = bass.AP(tile_c.tensor, tile_c.offset + 2,
                      [[tile_c.ap[0][0], 128], [128, 4], [4, 32]])
        scrA = bass.AP(tile_d.tensor, tile_d.offset,
                       [[tile_d.ap[0][0], 128], [32, 4], [1, 32]])
        k.op(V, lambda e: e.tensor_mul(re_, ke, cB), reads=["tile_b", "cosq"],
             writes=["tile_c"])
        k.op(V, lambda e: e.tensor_mul(scrA, ko_, sB), reads=["tile_b", "sinq"],
             writes=["tile_d"])
        k.op(V, lambda e: e.tensor_sub(re_, re_, scrA), reads=["tile_c", "tile_d"],
             writes=["tile_c"])
        k.op(V, lambda e: e.tensor_mul(ro_, ke, sB), reads=["tile_b", "sinq"],
             writes=["tile_c"])
        k.op(V, lambda e: e.tensor_mul(scrA, ko_, cB), reads=["tile_b", "cosq"],
             writes=["tile_d"])
        k.op(V, lambda e: e.tensor_add(ro_, ro_, scrA), reads=["tile_c", "tile_d"],
             writes=["tile_c"])
        for kc in range(2):
            kslice = bass.AP(tile_c.tensor, tile_c.offset + 256 * kc,
                             [[tile_c.ap[0][0], 128], [2, 128]])
            k.op(TE, lambda e, kslice=kslice: e.transpose(psC[:, 0:128], kslice,
                                                          ident),
                 reads=["tile_c", "ident"], writes=["psC"])
            k.op(V, lambda e, kc=kc, t=t: e.tensor_copy(
                ktl0[:, kc * T + 128 * t: kc * T + 128 * t + 128], psC[:, 0:128]),
                 reads=["psC"], writes=["ktl0"])
        for half in range(2):
            k.op(TE, lambda e, t=t, half=half: e.matmul(
                psB[:, ts(half, 512)], qlat2[:, ts(t, 128)],
                wqb_s[:, ts(half, 512)], start=True, stop=True),
                 reads=["qlat", "wqb"], writes=[PSB[half]])
        for half in range(2):
            k.op(V, lambda e, half=half: e.tensor_copy(tile_b[:, ts(half, 512)],
                                                       psB[:, ts(half, 512)]),
                 reads=[PSB[half]], writes=["tile_b"])
        emit_rope(tile_b, "tile_b", tile_c, "tile_c", t, NH)
        emit_transpose_tile(tile_c, "tile_c", qTt, "qTt")
        emit_attn_chunk(t, 0)
        emit_transpose_tile(tile_a, "tile_a", tile_b, "tile_b")
        for mc in range(8):
            w = wsm[mc % 2]
            wk = f"wsm{mc % 2}"
            k.dma(SY, w, bass.AP(wout0_d.tensor, 128 * mc,
                                 [[D, 128], [128 * D, 8], [1, 128]]),
                  writes=[wk])
            pairs = [(w[:, ts(c, 128)], tile_b[:, ts(c, 128)]) for c in range(8)]
            ps, pk = (psC, "psC") if mc % 2 == 0 else (psD, "psD")
            mm_group(ps[:, 0:128], pairs, pk, [wk, "tile_b"])
            k.op(V, lambda e, mc=mc, ps=ps: e.tensor_scalar(
                tile_c[:, ts(mc, 128)], ps[:, 0:128], bout0_s[:, mc:mc + 1], None,
                ALU.add),
                 reads=[pk, "bout0"], writes=["tile_c"])
        emit_transpose_tile(tile_c, "tile_c", tile_d, "tile_d")
        k.dma(SY, tile_a, x_d[ts(t, 128), :], writes=["tile_a"])
        k.op(V, lambda e, t=t: e.tensor_mul(tile_d, tile_d, A1[:, ts(t, 1024)]),
             reads=["tile_d", "A1"], writes=["tile_d"])
        k.op(V, lambda e: e.tensor_add(tile_d, tile_d, tile_a),
             reads=["tile_d", "tile_a"], writes=["tile_d"])
        k.dma(SY, x1sp[ts(t, 128), :], tile_d, reads=["tile_d"], writes=["x1sp"])
        k.op(V, lambda e: e.reduce_sum(stats[:, 0:1], tile_d, axis=AX.X),
             reads=["tile_d"], writes=["st0"])
        k.op(S, lambda e: e.activation(tile_c, tile_d, AF.Square,
                                       accum_out=stats[:, 1:2]),
             reads=["tile_d"], writes=["tile_c", "st1"])
        k.op(V, lambda e, t=t: e.tensor_scalar(lnm[:, 2 * t:2 * t + 1], stats[:, 0:1],
                                               1.0 / D, None, ALU.mult),
             reads=["st0"], writes=["lnm"])
        k.op(V, lambda e, t=t: e.tensor_scalar(stats[:, 4:5], lnm[:, 2 * t:2 * t + 1],
                                               lnm[:, 2 * t:2 * t + 1], None,
                                               ALU.mult),
             reads=["lnm"], writes=["st4"])
        k.op(V, lambda e: e.tensor_scalar(stats[:, 5:6], stats[:, 1:2], 1.0 / D, 1e-5,
                                          ALU.mult, ALU.add),
             reads=["st1"], writes=["st5"])
        k.op(V, lambda e: e.tensor_sub(stats[:, 5:6], stats[:, 5:6], stats[:, 4:5]),
             reads=["st5", "st4"], writes=["st5"])
        k.op(V, lambda e: e.reciprocal(stats[:, 6:7], stats[:, 5:6]),
             reads=["st5"], writes=["st6"])
        k.op(S, lambda e, t=t: e.activation(lnm[:, 2 * t + 1:2 * t + 2],
                                            stats[:, 6:7], AF.Sqrt),
             reads=["st6"], writes=["lnm"])
        k.op(V, lambda e, t=t: e.tensor_scalar(tile_a, tile_d,
                                               lnm[:, 2 * t:2 * t + 1],
                                               lnm[:, 2 * t + 1:2 * t + 2],
                                               ALU.subtract, ALU.mult),
             reads=["tile_d", "lnm"], writes=["tile_a"])
        emit_transpose_tile(tile_a, "tile_a", tile_c, "tile_c")
        k.dma(SY, xn1sp[ts(t, 128), :], tile_c, reads=["tile_c"], writes=["xn1sp"])

    k.fence(["ktl0", "vl0", "cosq", "sinq", "qlat", "kvlat", "wqb", "wkvb"],
            ["A2"])
    vbank = ([psA[:, ts(i, 512)] for i in range(4)]
             + [psB[:, ts(i, 512)] for i in range(2)] + [psC, psD])
    vbkey = PSA + PSB + ["psC", "psD"]
    for tsl in range(NTG):
        for i in range(4):
            k.dma(SY, tiles[:, ts(i, 1024)], xn1sp[ts(4 * tsl + i, 128), :],
                  reads=["xn1sp"], writes=[TILEK[i]])
        for mc in range(8):
            w = wsm[mc % 2]
            wk = f"wsm{mc % 2}"
            k.dma(SY, w, bass.AP(wkT_d.tensor, 128 * mc,
                                 [[D, 128], [128 * D, 8], [1, 128]]),
                  writes=[wk])
            ps, pk = (psC, "psC") if mc % 2 == 0 else (psD, "psD")
            pairs = [(w[:, ts(c, 128)], xnT_ap(tiles, c, 0, 512)) for c in range(8)]
            mm_group(ps, pairs, pk, [wk] + TILEK)
            k.op(V, lambda e, mc=mc, tsl=tsl, ps=ps: e.tensor_copy(
                A2[:, mc * T + 512 * tsl: mc * T + 512 * tsl + 512], ps),
                 reads=[pk], writes=["A2"])
        for half in range(2):
            for c in range(8):
                w = wsm[c % 2]
                wk = f"wsm{c % 2}"
                k.dma(SY, w[:, 0:512], wvT_d[ts(c, 128), ts(half, 512)], writes=[wk])
                for i in range(4):
                    bi = 2 * i + half
                    k.op(TE, lambda e, w=w, c=c, i=i, bi=bi: e.matmul(
                        vbank[bi], tiles[:, 1024 * i + 128 * c: 1024 * i + 128 * c + 128],
                        w[:, 0:512], start=(c == 0), stop=(c == 7)),
                         reads=[wk, TILEK[i]], writes=[vbkey[bi]])
        for i in range(4):
            for half in range(2):
                bi = 2 * i + half
                k.op(V, lambda e, tsl=tsl, i=i, half=half, bi=bi: e.tensor_copy(
                    A1[:, 1024 * (4 * tsl + i) + 512 * half:
                       1024 * (4 * tsl + i) + 512 * half + 512], vbank[bi]),
                     reads=[vbkey[bi]], writes=["A1"])

    for t in range(NT):
        k.dma(SY, stg[0], xn1sp[ts(t, 128), :], reads=["xn1sp"], writes=["stg0"])
        for mc in range(8):
            w = wsm[mc % 2]
            wk = f"wsm{mc % 2}"
            k.dma(SY, w, bass.AP(wqT_d.tensor, 128 * mc,
                                 [[D, 128], [128 * D, 8], [1, 128]]),
                  writes=[wk])
            ps, pk = (psC, "psC") if mc % 2 == 0 else (psD, "psD")
            pairs = [(w[:, ts(c, 128)], stg[0][:, ts(c, 128)]) for c in range(8)]
            mm_group(ps[:, 0:128], pairs, pk, [wk, "stg0"])
            k.op(V, lambda e, mc=mc, ps=ps: e.tensor_copy(qTt[:, ts(mc, 128)],
                                                          ps[:, 0:128]),
                 reads=[pk], writes=["qTt"])
        emit_attn_chunk(t, 1)
        emit_transpose_tile(tile_a, "tile_a", tile_b, "tile_b")
        for mc in range(8):
            w = wsm[mc % 2]
            wk = f"wsm{mc % 2}"
            k.dma(SY, w, bass.AP(woT_d.tensor, 128 * mc,
                                 [[D, 128], [128 * D, 8], [1, 128]]),
                  writes=[wk])
            pairs = [(w[:, ts(c, 128)], tile_b[:, ts(c, 128)]) for c in range(8)]
            ps, pk = (psC, "psC") if mc % 2 == 0 else (psD, "psD")
            mm_group(ps[:, 0:128], pairs, pk, [wk, "tile_b"])
            k.op(V, lambda e, mc=mc, ps=ps: e.tensor_copy(tile_c[:, ts(mc, 128)],
                                                          ps[:, 0:128]),
                 reads=[pk], writes=["tile_c"])
        emit_transpose_tile(tile_c, "tile_c", tile_d, "tile_d")
        k.dma(SY, ssp[ts(t, 128), :], tile_d, reads=["tile_d"], writes=["ssp"])

    k.fence(["A1"], P9C_A1)
    k.fence(["A2"], P9C_A2)
    decsl = [(p, off, w) for p in range(P_TREE)
             for off, w in ((0, 512), (512, 512), (1024, 340))]
    l5sl = [(p, 1364 + 512 * j, 512) for p in range(P_TREE) for j in range(8)]
    decck = [(p, o, w) for p in range(P_TREE)
             for o, w in [(128 * i, 128) for i in range(10)] + [(1280, 84)]]
    NCK = len(decck) + len(l5sl) * 4

    for t in range(NT):
        k.dma(SY, xn1t, xn1sp[ts(t, 128), :], reads=["xn1sp"], writes=["xn1t"])
        for si, (p, off, w) in enumerate(decsl):
            hid0 = p * PBLK + off
            wb = wbufA[si % 2]
            wbk = f"wbufA{si % 2}"
            k.dma(SY, bass.AP(wb.tensor, wb.offset, [[wb.ap[0][0], 128], [w, 8], [1, w]]),
                  bass.AP(w1T_d.tensor, hid0, [[HID1, 128], [128 * HID1, 8], [1, w]]),
                  writes=[wbk])
            bs = bslc[si % 2]
            bsk = f"bslc{si % 2}"
            k.dma(SY, bs[:, 0:w], b1bc_d[:, hid0:hid0 + w], writes=[bsk])
            ps, pk = (psC, "psC") if si % 2 == 0 else (psD, "psD")
            pairs = [(xn1t[:, ts(c, 128)], wb[:, c * w:(c + 1) * w]) for c in range(8)]
            mm_group(ps[:, 0:w], pairs, pk, ["xn1t", wbk])
            asb = aslc[si % 2]
            ask = f"aslc{si % 2}"
            k.op(V, lambda e, ps=ps, bs=bs, w=w, asb=asb: e.tensor_add(
                asb[:, 0:w], ps[:, 0:w], bs[:, 0:w]),
                 reads=[pk, bsk], writes=[ask])
            k.op(S, lambda e, p=p, off=off, w=w, asb=asb: e.activation(
                actdec[:, p * DEC_W + off: p * DEC_W + off + w], asb[:, 0:w],
                AF.Gelu),
                 reads=[ask], writes=["actdec"])

        def l5_ffn1(lsi):
            p, off, w = l5sl[lsi]
            hid0 = p * PBLK + off
            wb = wbufA[lsi % 2]
            wbk = f"wbufA{lsi % 2}"
            k.dma(SY, bass.AP(wb.tensor, wb.offset,
                              [[wb.ap[0][0], 128], [w, 8], [1, w]]),
                  bass.AP(w1T_d.tensor, hid0,
                          [[HID1, 128], [128 * HID1, 8], [1, w]]),
                  writes=[wbk])
            bs = bslc[lsi % 2]
            bsk = f"bslc{lsi % 2}"
            k.dma(SY, bs[:, 0:w], b1bc_d[:, hid0:hid0 + w], writes=[bsk])
            ps, pk = (psC, "psC") if lsi % 2 == 0 else (psD, "psD")
            pairs = [(xn1t[:, ts(c, 128)], wb[:, c * w:(c + 1) * w])
                     for c in range(8)]
            mm_group(ps[:, 0:w], pairs, pk, ["xn1t", wbk])
            asb = aslc[lsi % 4]
            ask = f"aslc{lsi % 4}"
            k.op(V, lambda e, ps=ps, bs=bs, w=w, asb=asb: e.tensor_add(
                asb[:, 0:w], ps[:, 0:w], bs[:, 0:w]),
                 reads=[pk, bsk], writes=[ask])
            k.op(S, lambda e, asb=asb, w=w: e.activation(asb[:, 0:w], asb[:, 0:w],
                                                         AF.Gelu),
                 reads=[ask], writes=[ask])

        for lsi in range(4):
            l5_ffn1(lsi)
        AD = actdec

        def A_f(f):
            return bass.AP(AD.tensor, AD.offset + f,
                           [[AD.ap[0][0], 128], [DEC_W, 4], [4, DEC_G]])
        k.op(V, lambda e: e.tensor_max(trm, A_f(0), A_f(1)),
             reads=["actdec"], writes=["trm"])
        k.op(V, lambda e: e.tensor_max(trm, trm, A_f(2)),
             reads=["trm", "actdec"], writes=["trm"])
        k.op(V, lambda e: e.tensor_max(trm, trm, A_f(3)),
             reads=["trm", "actdec"], writes=["trm"])
        for f in range(4):
            k.op(V, lambda e, f=f: e.tensor_tensor(tre[f], A_f(f), trm, ALU.is_equal),
                 reads=["actdec", "trm"], writes=[f"tre{f}"])
        k.op(V, lambda e: e.tensor_scalar(trc, tre[0], -1.0, 1.0, ALU.mult, ALU.add),
             reads=["tre0"], writes=["trc"])
        for f in range(1, 4):
            k.op(V, lambda e, f=f: e.tensor_mul(trt, tre[f], trc),
                 reads=[f"tre{f}", "trc"], writes=["trt"])
            if f < 3:
                k.op(V, lambda e, f=f: e.tensor_scalar(tre[f], tre[f], -1.0, 1.0,
                                                       ALU.mult, ALU.add),
                     reads=[f"tre{f}"], writes=[f"tre{f}"])
                k.op(V, lambda e, f=f: e.tensor_mul(trc, trc, tre[f]),
                     reads=["trc", f"tre{f}"], writes=["trc"])
            t_copy(V, tre[f], trt, ["trt"], [f"tre{f}"])
        M = mask1h
        k.op(V, lambda e: e.memset(
            bass.AP(M.tensor, M.offset, [[M.ap[0][0], 128], [G_TREE, 4], [1, 1]]),
            1.0),
             writes=["mask1h"])
        plat = 0
        for d in range(5):
            sz = 4 ** d
            for j in range(4):
                wprev = bass.AP(M.tensor, M.offset + plat,
                                [[M.ap[0][0], 128], [G_TREE, 4], [1, sz]])
                hj = bass.AP(treb.tensor, treb.offset + j * (P_TREE * DEC_G) + plat,
                             [[treb.ap[0][0], 128], [DEC_G, 4], [1, sz]])
                out = bass.AP(M.tensor, M.offset + plat + sz + j,
                              [[M.ap[0][0], 128], [G_TREE, 4], [4, sz]])
                k.op(V, lambda e, out=out, wprev=wprev, hj=hj: e.tensor_tensor(
                    out, wprev, hj, ALU.mult),
                     reads=["mask1h", f"tre{j}"], writes=["mask1h"])
            plat += sz
        mdec = bass.AP(M.tensor, M.offset,
                       [[M.ap[0][0], 128], [G_TREE, 4], [1, DEC_G], [0, 4]])
        adec4 = bass.AP(AD.tensor, AD.offset,
                        [[AD.ap[0][0], 128], [DEC_W, 4], [4, DEC_G], [1, 4]])
        k.op(V, lambda e: e.tensor_tensor(adec4, adec4, mdec, ALU.mult),
             reads=["actdec", "mask1h"], writes=["actdec"])
        nck = 0
        sib = 0
        for p, o, w in decck:
            k.op(TE, lambda e, p=p, o=o, w=w: e.transpose(
                psC[0:w, 0:128], actdec[:, p * DEC_W + o: p * DEC_W + o + w], ident),
                 reads=["actdec", "ident"], writes=["psC"])
            ms = mstg[sib % 2]
            msk = f"mstg{sib % 2}"
            k.op(V, lambda e, w=w, ms=ms: e.tensor_copy(ms[0:w, 0:128],
                                                        psC[0:w, 0:128]),
                 reads=["psC"], writes=[msk])
            wb = wbufB[sib % 2]
            wbk = f"wbufB{sib % 2}"
            hid0 = p * PBLK + o
            k.dma(SY, wb[0:w, 0:1024] if w == 128 else
                  bass.AP(wb.tensor, wb.offset, [[wb.ap[0][0], w], [1, 1024]]),
                  w2T_d[hid0:hid0 + w, :], writes=[wbk])
            for half in range(2):
                k.op(TE, lambda e, ms=ms, wb=wb, w=w, half=half, nck=nck: e.matmul(
                    psB[:, ts(half, 512)], ms[0:w, 0:128], wb[0:w, ts(half, 512)],
                    start=(nck == 0), stop=(nck == NCK - 1)),
                     reads=[msk, wbk], writes=[PSB[half]])
            nck += 1
            sib += 1
        for lsi in range(4, 36):
            lsr = lsi - 4
            p, off, w = l5sl[lsr]
            hid0 = p * PBLK + off
            asb = aslc[lsr % 4]
            ask = f"aslc{lsr % 4}"
            g0 = DEC_G + (off - DEC_W) // 4
            mB = bass.AP(M.tensor, M.offset + p * G_TREE + g0,
                         [[M.ap[0][0], 128], [1, w // 4], [0, 4]])
            k.op(V, lambda e, asb=asb, mB=mB, w=w: e.tensor_tensor(
                asb[:, 0:w], asb[:, 0:w], mB, ALU.mult),
                 reads=[ask, "mask1h"], writes=[ask])
            for blk in range(4):
                k.op(TE, lambda e, asb=asb, blk=blk: e.transpose(
                    psA[:, ts(blk, 128)], asb[:, ts(blk, 128)], ident),
                     reads=[ask, "ident"], writes=["psA0"])
            ms = mstg[lsr % 2]
            msk = f"mstg{lsr % 2}"
            k.op(V, lambda e, ms=ms: e.tensor_copy(ms, psA[:, 0:512]),
                 reads=["psA0"], writes=[msk])
            wb2 = wbufB[lsr % 2]
            wb2k = f"wbufB{lsr % 2}"
            k.dma(SY, bass.AP(wb2.tensor, wb2.offset,
                              [[wb2.ap[0][0], 128], [1024, 4], [1, 1024]]),
                  bass.AP(w2T_d.tensor, hid0 * D, [[D, 128], [128 * D, 4], [1, 1024]]),
                  writes=[wb2k])
            for blk in range(4):
                for half in range(2):
                    k.op(TE, lambda e, ms=ms, wb2=wb2, blk=blk, half=half, nck=nck:
                         e.matmul(psB[:, ts(half, 512)], ms[:, ts(blk, 128)],
                                  wb2[:, 1024 * blk + 512 * half:
                                      1024 * blk + 512 * (half + 1)],
                                  start=(nck == 0), stop=(nck == NCK - 1)),
                         reads=[msk, wb2k], writes=[PSB[half]])
                nck += 1
            if lsi < 32:
                l5_ffn1(lsi)
        assert nck == NCK
        for half in range(2):
            k.op(S, lambda e, half=half: e.activation(tile_c[:, ts(half, 512)],
                                                      psB[:, ts(half, 512)], AF.Silu),
                 reads=[PSB[half]], writes=["tile_c"])
        k.dma(SY, tile_a, x1sp[ts(t, 128), :], reads=["x1sp"], writes=["tile_a"])
        k.dma(SY, tile_b, ssp[ts(t, 128), :], reads=["ssp"], writes=["tile_b"])
        k.op(V, lambda e: e.tensor_mul(tile_c, tile_c, tile_b),
             reads=["tile_c", "tile_b"], writes=["tile_c"])
        k.op(V, lambda e: e.tensor_add(tile_c, tile_c, tile_a),
             reads=["tile_c", "tile_a"], writes=["tile_c"])
        k.dma(SY, out_d[ts(t, 128), :], tile_c, reads=["tile_c"], writes=["outd"])

    stack = k.build()
    return nc, stack


def prep_shared(inputs):
    def f32(a):
        return np.ascontiguousarray(np.asarray(a, dtype=np.float32))
    wqb = np.asarray(inputs["L0_attn_Wqb"])
    return dict(
        wqa=f32(inputs["L0_attn_Wqa"]),
        wqb=f32(wqb[:, l0_wqb_perm()]),
        wkva=f32(inputs["L0_attn_Wkva"]),
        wkvb=f32(inputs["L0_attn_Wkvb"]),
        wout0=f32(inputs["L0_attn_Wout"]),
        bout0=f32(np.asarray(inputs["L0_attn_bout"]).reshape(8, 128).T),
        winT0=f32(np.asarray(inputs["L0_ffn_W_in"]).T),
        bin0=f32(np.asarray(inputs["L0_ffn_b_in"]).reshape(32, 128).T),
        wo0=f32(inputs["L0_ffn_W_out"]),
        bo0bc=f32(np.broadcast_to(np.asarray(inputs["L0_ffn_b_out"],
                                             dtype=np.float32)[None, :], (128, D))),
        wqT=f32(np.asarray(inputs["L1_attn_WQ"]).T),
        wkT=f32(np.asarray(inputs["L1_attn_WK"]).T),
        wvT=f32(np.asarray(inputs["L1_attn_WV"]).T),
        woT=f32(np.asarray(inputs["L1_attn_WO"]).T),
        w1T=f32(np.asarray(inputs["L1_ffn_W_in"]).T),
        b1bc=f32(np.broadcast_to(np.asarray(inputs["L1_ffn_b_in"],
                                            dtype=np.float32)[None, :], (128, HID1))),
        w2T=f32(np.asarray(inputs["L1_ffn_W_out"]).T),
        ident=np.eye(128, dtype=np.float32),
    )


def kernel(**inputs):
    T = 2048
    n_cores = 8
    x = np.ascontiguousarray(np.asarray(inputs["L0_attn_x"], dtype=np.float32))
    shared = prep_shared(inputs)
    nc, stack = build(T=T, n_cores=n_cores)
    in_maps = [dict(shared, x=np.ascontiguousarray(x[c])) for c in range(n_cores)]
    res = run_bass_kernel_spmd(nc, in_maps, core_ids=list(range(n_cores)))
    stack.close()
    return np.stack([np.asarray(r["out"], dtype=np.float32) for r in res.results])
